# version 06
# speedup vs baseline: 1.4704x; 1.0074x over previous
import contextlib

import numpy as np

import concourse.bass as bass
import concourse.mybir as mybir
from concourse.bass_utils import run_bass_kernel_spmd

F32 = mybir.dt.float32
BF16 = mybir.dt.bfloat16
AF = mybir.ActivationFunctionType
ALU = mybir.AluOpType
AX = mybir.AxisListType

D = 1024
DI = 2048
DS = 16
DTR = 64
DC = 4
NB = 2
L = 8192
TOK = 2048
TC = 256
NCH = TOK // TC
SG = 4
NSG = DS // SG
NCT = DI // 128
KD = D // 128
EPS = 1e-5

ENGINES = ("sync", "tensor", "vector", "scalar", "gpsimd")
_DBG = []
_TRUNC = 0
_NC_CACHE = {}
SMALL_COLS = []


def _dedup(waits):
    best = {}
    for s, tgt in waits:
        best[s] = max(best.get(s, 0), tgt)
    return list(best.items())


class Emitter:
    def __init__(self):
        self.streams = {e: [] for e in ENGINES}
        self.count = {}
        self.dma_sems = set()

    def sem(self, name):
        self.count.setdefault(name, 0)
        return name

    def emit(self, eng, fn, waits=(), inc=None, inc_n=1):
        fixed = []
        for s, tgt in waits:
            assert tgt <= self.count[s], (
                f"forward wait: {eng} waits {s}>={tgt} emitted={self.count[s]}")
            if s in self.dma_sems:
                tgt = self.count[s]
            fixed.append((s, tgt))
        waits = _dedup(fixed)
        self.streams[eng].append((fn, tuple(waits), inc, inc_n))
        if inc is not None:
            self.count[inc] += inc_n
            return (inc, self.count[inc])
        return None

    def now(self, sem):
        return (sem, self.count[sem])


class SbufPool:
    def __init__(self, nc, size=200 * 1024):
        self.nc = nc
        backing = nc.alloc_sbuf_tensor("manualpool", [128, size], mybir.dt.uint8)
        base = nc.lookup_mloc(backing).addr
        self.off = base
        self.budget = base + size
        self.n = 0
        self.offsets = {}

    def _mk(self, name, shape, dtype, off):
        self.n += 1
        self.offsets[name] = off
        h = self.nc.alloc_sbuf_tensor_at(f"{name}_{self.n}", list(shape), dtype,
                                         offset=off)
        return h.ap()

    def alloc(self, name, shape, dtype):
        per_part = int(np.prod(shape[1:])) * mybir.dt.size(dtype)
        per_part = (per_part + 31) // 32 * 32
        off = self.off
        self.off += per_part
        assert self.off <= self.budget, (
            f"SBUF overflow at {name}: {self.off} > {self.budget}")
        return self._mk(name, shape, dtype, off)

    def alloc_at(self, name, shape, dtype, off):
        per_part = int(np.prod(shape[1:])) * mybir.dt.size(dtype)
        assert off + per_part <= self.budget, f"overlay overflow {name}"
        return self._mk(name, shape, dtype, off)


def _f32(x):
    return np.ascontiguousarray(x, dtype=np.float32)


def _lhsT_lay(w_T):
    K, M = w_T.shape
    if K >= 128:
        kc = K // 128
        return _f32(w_T.reshape(kc, 128, M).transpose(1, 0, 2).reshape(128, kc * M))
    return _f32(w_T)


def host_prepare(inputs):
    x = np.asarray(inputs["L0_attn_x"])

    def col_feat(v):
        return _f32(np.asarray(v).reshape(KD, 128).T)

    def col_inner(v):
        return _f32(np.asarray(v).reshape(NCT, 128).T)

    shared = {}
    for d in ("f", "b"):
        shared[f"in_wT_{d}"] = _lhsT_lay(np.asarray(inputs[f"L0_attn_in_w_{d}"]).T)
        shared[f"out_wT_{d}"] = _lhsT_lay(np.asarray(inputs[f"L0_attn_out_w_{d}"]).T)
        shared[f"xp_wT_{d}"] = _lhsT_lay(np.asarray(inputs[f"L0_attn_xp_w_{d}"]).T)
        shared[f"dtp_wT_{d}"] = _lhsT_lay(np.asarray(inputs[f"L0_attn_dtp_w_{d}"]).T)
        cw = np.asarray(inputs[f"L0_attn_conv_w_{d}"]).reshape(DI, DC)
        shared[f"conv_w_{d}"] = _f32(
            cw.reshape(NCT, 128, DC).transpose(1, 0, 2).reshape(128, NCT * DC))
        al = np.asarray(inputs[f"L0_attn_A_log_{d}"])
        shared[f"A_log_{d}"] = _f32(
            al.reshape(NCT, 128, DS).transpose(1, 0, 2).reshape(128, NCT * DS))
    shared["proj_wT"] = _lhsT_lay(np.asarray(inputs["L0_attn_proj_w"]).T)
    shared["cn_w1"] = _lhsT_lay(np.asarray(inputs["L0_ffn_cn_w1"]))
    shared["cn_w2"] = _lhsT_lay(np.asarray(inputs["L0_ffn_cn_w2"]))
    shared["cn_w3"] = _lhsT_lay(np.asarray(inputs["L0_ffn_cn_w3"]))
    w4 = np.asarray(inputs["L0_ffn_cn_w4"])
    shared["cn_w4"] = _lhsT_lay(np.concatenate([w4, w4], axis=1))
    shared["gc_w1"] = _lhsT_lay(np.asarray(inputs["L0_ffn_gc_w1"]))
    shared["gc_w2"] = _lhsT_lay(np.asarray(inputs["L0_ffn_gc_w2"]))
    w3g = np.asarray(inputs["L0_ffn_gc_w3"])
    shared["gc_w3"] = _lhsT_lay(np.concatenate([w3g, w3g], axis=1))
    sel = np.zeros((128, DS * 128), np.float32)
    for s in range(DS):
        sel[s, s * 128:(s + 1) * 128] = 1.0
    shared["sel"] = sel

    cols = {}
    cols["ln_g"] = col_feat(inputs["L0_attn_ln_gamma"])
    cols["ln_b"] = col_feat(inputs["L0_attn_ln_beta"])
    cols["proj_b"] = col_feat(inputs["L0_attn_proj_b"])
    for d in ("f", "b"):
        cols[f"conv_b_{d}"] = col_inner(inputs[f"L0_attn_conv_b_{d}"])
        cols[f"dtp_b_{d}"] = col_inner(inputs[f"L0_attn_dtp_b_{d}"])
        cols[f"Dk_{d}"] = col_inner(inputs[f"L0_attn_D_{d}"])
    cols["cn_b1"] = col_feat(inputs["L0_ffn_cn_b1"])
    cols["cn_b2"] = _f32(np.asarray(inputs["L0_ffn_cn_b2"]).reshape(4, 128).T)
    cols["cn_b3"] = _f32(np.asarray(inputs["L0_ffn_cn_b3"]).reshape(2, 128).T)
    cols["cn_b4"] = np.full(
        (2, 1), float(np.asarray(inputs["L0_ffn_cn_b4"]).reshape(-1)[0]),
        np.float32)
    cols["gc_b1"] = _f32(np.asarray(inputs["L0_ffn_gc_b1"]).reshape(4, 128).T)
    cols["gc_b2"] = _f32(np.asarray(inputs["L0_ffn_gc_b2"]).reshape(2, 128).T)
    cols["gc_b3"] = np.full(
        (2, 1), float(np.asarray(inputs["L0_ffn_gc_b3"]).reshape(-1)[0]),
        np.float32)
    cols["eps"] = np.full((128, 1), EPS, np.float32)
    cols["one"] = np.full((128, 1), 1.0, np.float32)
    cols["ar"] = np.full(
        (2, 1), float(np.asarray(inputs["L0_ffn_adaptive_range"]).reshape(-1)[0]),
        np.float32)

    maps = []
    for core in range(8):
        beta, q = core // 4, core % 4
        sl = x[beta, q * TOK:(q + 1) * TOK]
        prev3 = x[beta, q * TOK - 3:q * TOK] if q > 0 else x[beta, 0:3]
        nxt3 = (x[beta, (q + 1) * TOK:(q + 1) * TOK + 3] if q < 3
                else x[beta, L - 3:L])

        def featmaj(a):
            T = a.shape[0]
            return _f32(a.T.reshape(KD, 128, T).transpose(1, 0, 2)
                        .reshape(128, KD * T))

        percols = dict(cols)
        percols["bmask_f"] = np.full((128, 1), 0.0 if q == 0 else 1.0, np.float32)
        percols["bmask_b"] = np.full((128, 1), 0.0 if q == 3 else 1.0, np.float32)
        oh_f = np.zeros((128, 4), np.float32)
        if q >= 1:
            oh_f[:, q] = 1.0
        percols["oh_f"] = oh_f
        oh_b = np.zeros((128, 4), np.float32)
        if q <= 2:
            oh_b[:, 3 - q] = 1.0
        percols["oh_b"] = oh_b

        global SMALL_COLS
        if not SMALL_COLS:
            SMALL_COLS = [(k, percols[k].shape[1]) for k in percols]
        small = np.zeros((128, sum(w for _, w in SMALL_COLS)), np.float32)
        o = 0
        for k, w in SMALL_COLS:
            a = percols[k]
            small[: a.shape[0], o:o + w] = a
            o += w

        m = dict(shared)
        m["x_f"] = featmaj(np.concatenate([prev3, sl], 0))
        m["x_b"] = featmaj(np.concatenate([nxt3[::-1], sl[::-1]], 0))
        m["small"] = small
        maps.append(m)
    return maps


def small_index():
    idx = {}
    o = 0
    for k, w in SMALL_COLS:
        idx[k] = (o, w)
        o += w
    return idx, o



def build_nc():
    SIX, NSM = small_index()
    nc = bass.Bass("TRN2", target_bir_lowering=False)
    em = Emitter()
    for s in ("ld", "sp", "t", "v", "s", "cc"):
        em.sem(s)

    dram = {}

    def din(name, shape, dtype=F32):
        dram[name] = nc.dram_tensor(name, list(shape), dtype,
                                    kind="ExternalInput").ap()

    def dint(name, shape, dtype=F32):
        dram[name] = nc.dram_tensor(name, list(shape), dtype, kind="Internal").ap()

    din("x_f", [128, KD * (TOK + 3)])
    din("x_b", [128, KD * (TOK + 3)])
    din("small", [128, NSM])
    din("sel", [128, DS * 128])
    for d in ("f", "b"):
        din(f"in_wT_{d}", [128, KD * 2 * DI])
        din(f"out_wT_{d}", [128, NCT * D])
        din(f"xp_wT_{d}", [128, NCT * 96])
        din(f"dtp_wT_{d}", [DTR, DI])
        din(f"conv_w_{d}", [128, NCT * DC])
        din(f"A_log_{d}", [128, NCT * DS])
        dint(f"s_xc_{d}", [128, NCT * TOK])
        dint(f"s_sz_{d}", [128, NCT * TOK])
        dint(f"s_dt_{d}", [DTR, TOK], BF16)
        dint(f"s_bc_{d}", [16, 2 * TOK], BF16)
        dint(f"s_yp_{d}", [128, KD * TOK], BF16)
        dint(f"cc_in_{d}", [128, 2 * NCT * DS])
        dint(f"cc_out_{d}", [512, 2 * NCT * DS])
    din("proj_wT", [128, NCT * D])
    din("cn_w1", [128, KD * D])
    din("cn_w2", [128, KD * 512])
    din("cn_w3", [128, 4 * 256])
    din("cn_w4", [128, 2 * 2])
    din("gc_w1", [128, KD * 512])
    din("gc_w2", [128, 4 * 256])
    din("gc_w3", [128, 2 * 2])
    dint("s_ffn", [128, KD * TOK])
    dint("cc_in_m", [128, KD])
    dint("cc_out_m", [128, KD])
    dram["out"] = nc.dram_tensor("out", [128, KD * TOK], F32,
                                 kind="ExternalOutput").ap()
    dbg = {}
    for name, shape, dtype in _DBG:
        dbg[name] = nc.dram_tensor("dbg_" + name, list(shape), dtype,
                                   kind="ExternalOutput").ap()
        em.sem("sp")

    with contextlib.ExitStack() as stack:
        psum = [stack.enter_context(nc.psum_tensor(f"ps{i}", [128, 512],
                                                   F32)).ap()
                for i in range(8)]
        _Prog(nc, em, dram, dbg, psum, SIX, NSM).build()
    return nc


class _Prog:
    def __init__(self, nc, em, dram, dbg, psum, SIX, NSM):
        self.nc, self.em, self.dram, self.dbg = nc, em, dram, dbg
        self.psum = psum
        self.SIX, self.NSM = SIX, NSM
        self.ev = {}
        self.bank_w = [None] * 8
        self.bank_r = [None] * 8
        self.free = {}
        self.sb = SbufPool(nc)

    def E(self, eng, fn, waits=(), inc=None, n=1):
        return self.em.emit(eng, fn, waits=[w for w in waits if w is not None],
                            inc=inc, inc_n=n)

    def dma(self, queue, out, in_, waits=(), sem=None):
        s = sem or ("ld" if queue == "sync" else "sp")
        self.em.sem(s)
        self.em.dma_sems.add(s)

        def fn(eng, out=out, in_=in_):
            return eng.dma_start(out=out, in_=in_)
        return self.em.emit(queue, fn,
                            waits=[w for w in waits if w is not None],
                            inc=s, inc_n=16)

    def sv(self, name):
        o, n = self.SIX[name]
        return self.small[:, o:o + n]

    def tap(self, name, ap, wait_ev=None, guard=None):
        if name in self.dbg:
            nc_ = int(np.prod(ap.shape[1:]))
            np_ = ap.shape[0]
            stage = self.px2[0:np_, 0:nc_] if ap.dtype == F32 else \
                self.pb1[0:np_, 0:nc_]
            self.E("vector", lambda v: v.tensor_copy(stage, ap),
                   waits=[wait_ev, self.free.get("tap")], inc="v")
            e = self.dma("gpsimd", self.dbg[name][:, :], stage,
                         waits=[self.em.now("v")], sem="sp_dbg")
            self.free["tap"] = e

    def build(self):
        self.alloc_all()
        self.emit_consts()
        if _TRUNC:
            self.emit_main("f")
            em = self.em
            allsems = [em.now(s) for s in sorted(em.count)
                       if s.startswith(("ld_", "sp_"))]
            self.E("gpsimd", lambda g: g.nop(), waits=allsems)
            self._replay()
            return
        self.emit_main("f")
        self.emit_xnsum_cc()
        self.emit_main("b")
        self.emit_ffn()
        self.emit_combine("f")
        self.emit_corr("f")
        self.emit_combine("b")
        self.emit_corr("b")
        self.emit_final()
        em = self.em
        allsems = [em.now(s) for s in sorted(em.count)
                   if s.startswith(("ld_", "sp_")) or s == "cc"]
        self.E("gpsimd", lambda g: g.nop(), waits=allsems)
        self._replay()

    def alloc_all(self):
        sb = self.sb
        self.small = sb.alloc("small", [128, self.NSM], F32)
        self.Asb = {d: sb.alloc(f"A_{d}", [128, NCT * DS], F32) for d in "fb"}
        self.convw = {d: sb.alloc(f"cw_{d}", [128, NCT * DC], F32) for d in "fb"}
        self.xpw = {d: sb.alloc(f"xpw_{d}", [128, NCT * 96], BF16) for d in "fb"}
        self.dtpw = {d: sb.alloc(f"dtpw_{d}", [DTR, DI], BF16) for d in "fb"}
        self.selw = sb.alloc("selw", [128, DS * 128], BF16)
        self.ones2 = sb.alloc("ones2", [2, 128], BF16)
        self.ones128 = sb.alloc("ones128", [128, 2], BF16)
        self.zerosT = sb.alloc("zerosT", [128, TC], F32)
        self.hstate = {d: sb.alloc(f"hst_{d}", [128, NCT * DS], F32) for d in "fb"}
        self.sumdl = {d: sb.alloc(f"sdl_{d}", [128, NCT], F32) for d in "fb"}
        self.hin = {d: sb.alloc(f"hin_{d}", [128, NCT * DS], F32) for d in "fb"}
        self.bound = {d: sb.alloc(f"bnd_{d}", [128, NCT * 4], F32) for d in "fb"}
        self.xnsum = sb.alloc("xnsum", [128, KD], F32)
        self.gfrow = sb.alloc("gfrow", [2, 8], F32)
        self.pcarry = sb.alloc("pcarry", [128, NCT], F32)
        self.red1 = sb.alloc("red1", [128, 1], F32)
        self.lnhl = sb.alloc("lnhl", [2, TC], BF16)
        self.lnlo = sb.alloc("lnlo", [2, TC], BF16)
        self.lnr = [sb.alloc(f"lnr_{i}", [2, TC], F32) for i in range(5)]
        self.tkt = sb.alloc("tkt", [128, NCT * DS], F32)
        self.cctmp = sb.alloc("cctmp", [128, 2 * NCT * DS], F32)
        self.hpre = [sb.alloc(f"hpre_{i}", [128, NCT * DS], F32) for i in (0, 1)]

        r0 = sb.off
        self.xch = sb.alloc("xch", [128, KD * TC], F32)
        self.xbf = sb.alloc("xbf", [128, KD * TC], BF16)
        self.xn2bf = sb.alloc("xn2bf", [128, KD * TC], BF16)
        self.xcbf = sb.alloc("xcbf", [128, NCT * TC], BF16)
        self.wstA = sb.alloc("wstA", [128, KD * 256], F32)
        assert sb.off - r0 == 32 * 1024, sb.off - r0
        self.bca = sb.alloc_at("bca", [128, 2 * DS * TC], F32, r0)
        self.sqbf = self.xcbf[:, 0:KD * TC]

        self.wstB = sb.alloc("wstB", [128, KD * 256], F32)
        self.wst32 = [self.wstA, self.wstB]
        self.wst16 = [sb.alloc(f"wst16_{i}", [128, KD * 256], BF16)
                      for i in (0, 1)]

        self.xe3 = sb.alloc("xe3", [128, KD * 3], F32)
        self.xbfe = sb.alloc("xbfe", [128, KD * 3], BF16)
        self.sqbfe = sb.alloc("sqbfe", [128, KD * 3], BF16)
        self.xn2e = sb.alloc("xn2e", [128, KD * 3], BF16)
        self.lnre = [sb.alloc(f"lnre_{i}", [2, 3], F32) for i in range(5)]
        self.lnhle = sb.alloc("lnhle", [2, 3], BF16)
        self.lnhle2 = sb.alloc("lnhle2", [2, 3], BF16)
        self.xcpre = [sb.alloc(f"xcpre_{i}", [128, TC + 3], F32) for i in (0, 1)]
        self.convt = [sb.alloc(f"convt_{i}", [128, TC], F32) for i in (0, 1)]
        self.xcring = [sb.alloc(f"xcr_{i}", [128, TC], F32) for i in (0, 1)]
        self.szring = [sb.alloc(f"szr_{i}", [128, TC], F32) for i in (0, 1)]
        self.xcrel = [sb.alloc(f"xcl_{i}", [128, TC], F32) for i in (0, 1)]
        self.szrel = [sb.alloc(f"szl_{i}", [128, TC], F32) for i in (0, 1)]
        self.dtbf = sb.alloc("dtbf", [DTR, TC], BF16)
        self.bhi = sb.alloc("bhi", [16, TC], BF16)
        self.blo = sb.alloc("blo", [16, TC], BF16)
        self.chi = sb.alloc("chi", [16, TC], BF16)
        self.clo = sb.alloc("clo", [16, TC], BF16)
        self.bclo = sb.alloc("bclo", [16, TC], F32)
        self.dlt = [sb.alloc(f"dlt_{i}", [128, TC], F32) for i in (0, 1)]
        self.sp1 = sb.alloc("sp1", [128, TC], F32)
        self.sp2 = sb.alloc("sp2", [128, TC], F32)
        self.dut = sb.alloc("dut", [128, TC], F32)
        a0 = sb.off
        self.ar_a2 = [sb.alloc(f"ar_a{i}", [128, SG * TC], F32) for i in (0, 1)]
        self.ffnsq = sb.alloc_at("ffnsq", [128, KD * TC], BF16, a0)
        self.ar_b = sb.alloc("ar_b", [128, SG * TC], F32)
        self.ar_h = sb.alloc("ar_h", [128, SG * TC], F32)
        self.ar_m = sb.alloc("ar_m", [128, SG * TC], F32)
        self.fold = sb.alloc("fold", [128, 2 * TC], F32)
        self.ett = [sb.alloc(f"ett_{i}", [128, SG * TC], F32) for i in (0, 1)]
        self.mtt = [sb.alloc(f"mtt_{i}", [128, TC], F32) for i in (0, 1)]
        self.yacc = [sb.alloc(f"yacc_{i}", [128, TC], F32) for i in (0, 1)]
        self.g1ch = sb.alloc("g1ch", [128, NCT * TC], BF16)
        self.ypst = [sb.alloc(f"ypst_{i}", [128, TC], BF16) for i in (0, 1)]
        self.px1 = sb.alloc("px1", [128, KD * TC], F32)
        self.px2 = sb.alloc("px2", [128, KD * TC], F32)
        self.pb1 = sb.alloc("pb1", [128, KD * TC], BF16)

    def emit_consts(self):
        E, dma, em, ev, free = self.E, self.dma, self.em, self.ev, self.free
        ev["small"] = dma("sync", self.small[:, :], self.dram["small"][:, :], sem="ld_aux")
        prev_aux = ev["small"]
        e = dma("sync", self.wstB[:, :DS * 128], self.dram["sel"][:, :], sem="ld_w1")
        E("vector", lambda v: v.tensor_copy(self.selw[:, :],
                                            self.wstB[:, :DS * 128]),
          waits=[e], inc="v")
        free["wst1"] = em.now("v")
        for d in "fb":
            prev_aux = dma("sync", self.convw[d][:, :],
                           self.dram[f"conv_w_{d}"][:, :], sem="ld_aux",
                           waits=[prev_aux])
            ea = dma("sync", self.Asb[d][:, :], self.dram[f"A_log_{d}"][:, :],
                     sem="ld_aux", waits=[prev_aux])
            prev_aux = ea
            e1 = dma("sync", self.wstB[:, :NCT * 96],
                     self.dram[f"xp_wT_{d}"][:, :], waits=[free.get("wst1")],
                     sem="ld_w1")
            E("vector", lambda v, d=d: v.tensor_copy(self.xpw[d][:, :],
                                                     self.wstB[:, :NCT * 96]),
              waits=[e1], inc="v")
            free["wst1"] = em.now("v")
            e2 = dma("sync", self.wstA[:DTR, :DI],
                     self.dram[f"dtp_wT_{d}"][:, :], waits=[free.get("wst0")],
                     sem="ld_w0")
            E("vector", lambda v, d=d: v.tensor_copy(self.dtpw[d][:, :],
                                                     self.wstA[:DTR, :DI]),
              waits=[e2], inc="v")
            free["wst0"] = em.now("v")
            E("scalar", lambda s, d=d: s.activation(self.Asb[d][:, :],
                                                    self.Asb[d][:, :], AF.Exp),
              waits=[ea], inc="s")
            E("vector", lambda v, d=d: v.tensor_scalar(
                self.Asb[d][:, :], self.Asb[d][:, :], -1.0, None, ALU.mult),
              waits=[em.now("s")], inc="v")
            ev[f"A_{d}"] = em.now("v")

        def vinit(v):
            v.memset(self.ones2[:, :], 1.0)
            v.memset(self.ones128[:, :], 1.0)
            v.memset(self.zerosT[:, :], 0.0)
            for d in "fb":
                v.memset(self.hstate[d][:, :], 0.0)
                v.memset(self.sumdl[d][:, :], 0.0)
                v.memset(self.hin[d][:, :], 0.0)
            v.memset(self.xnsum[:, :], 0.0)
            return v.memset(self.pcarry[:, :], 0.0)
        E("vector", vinit, inc="v")
        ev["init"] = em.now("v")
        E("vector", lambda v: v.memset(self.red1[:, :], 0.0),
          waits=[ev["small"]], inc="v")
        ev["svr"] = em.now("v")

    def emit_ln(self, xsrc, ncols, dst_xn, dst_xn2, rows, hl, do_sum,
               sA, sB, xb, sq, src_ev=None):
        E, em, ev, free = self.E, self.em, self.ev, self.free
        psum, bank_w, bank_r = self.psum, self.bank_w, self.bank_r
        E("vector", lambda v: v.tensor_copy(xb, xsrc),
          waits=[src_ev, free.get("stats_xb")], inc="v")
        cast_ev = em.now("v")
        def ssq(s):
            r = None
            for kc in range(KD):
                r = s.activation(sq[:, kc * ncols:(kc + 1) * ncols],
                                 xsrc[:, kc * ncols:(kc + 1) * ncols],
                                 AF.Square)
            return r
        E("scalar", ssq, waits=[src_ev, free.get("stats_sq")], inc="s")
        sq_ev = em.now("s")
        pa, pb = psum[sA], psum[sB]

        def mmstats(t):
            r = None
            for kc in range(KD):
                r = t.matmul(pa[0:2, :ncols], self.ones128[:, :],
                             xb[:, kc * ncols:(kc + 1) * ncols],
                             start=(kc == 0), stop=(kc == KD - 1))
            for kc in range(KD):
                r = t.matmul(pa[32:34, :ncols], self.ones128[:, :],
                             sq[:, kc * ncols:(kc + 1) * ncols],
                             start=(kc == 0), stop=(kc == KD - 1))
            return r
        if "xbfdbg" in self.dbg:
            E("vector", lambda v: v.tensor_copy(self.px1[:, 0:KD * ncols], xb),
              waits=[cast_ev, sq_ev, self.free.get("tap")], inc="v")
            self.tap("xbfdbg", self.px1[:, 0:KD * ncols], em.now("v"))
            E("vector", lambda v: v.tensor_copy(self.px1[:, 0:KD * ncols], sq),
              waits=[self.free.get("tap")], inc="v")
            self.tap("sqdbg", self.px1[:, 0:KD * ncols], em.now("v"))
            del self.dbg["xbfdbg"]
        E("tensor", mmstats, waits=[cast_ev, sq_ev, bank_r[sA], ev["init"]],
          inc="t")
        bank_w[sA] = em.now("t")
        free["stats_xb"] = bank_w[sA]
        free["stats_sq"] = bank_w[sA]
        mu = rows[0][:, :ncols]
        var = rows[1][:, :ncols]
        t3 = rows[2][:, :ncols]
        rs = rows[3][:, :ncols]
        rr = rows[4][:, :ncols]

        if "statsdbg" in self.dbg:
            def vst(v):
                v.tensor_copy(self.px1[0:2, 0:ncols], pa[0:2, :ncols])
                return v.tensor_copy(self.px1[0:2, TC:TC + ncols],
                                     pa[32:34, :ncols])
            E("vector", vst, waits=[bank_w[sA], self.free.get("tap")], inc="v")
            self.tap("statsdbg", self.px1[:, 0:2 * TC], em.now("v"))
            del self.dbg["statsdbg"]

        def vstats(v):
            v.tensor_scalar(mu, pa[0:2, :ncols], 1.0 / D, None, ALU.mult)
            v.tensor_scalar(var, pa[32:34, :ncols], 1.0 / D, None, ALU.mult)
            v.tensor_mul(t3, mu, mu)
            return v.tensor_sub(var, var, t3)
        E("vector", vstats, waits=[bank_w[sA]], inc="v")
        bank_r[sA] = em.now("v")
        E("scalar", lambda s: s.activation(t3, var, AF.Sqrt,
                                           bias=self.sv("eps")[0:2, :]),
          waits=[em.now("v"), ev["svr"]], inc="s")
        E("vector", lambda v: v.reciprocal(rs, t3), waits=[em.now("s")], inc="v")
        if dst_xn2 is not None:
            def vln2(v):
                v.tensor_mul(rr, rs, rs)
                return v.tensor_mul(rr, rr, var)
            E("vector", vln2, inc="v")
            E("scalar", lambda s: s.activation(t3, rr, AF.Sqrt,
                                               bias=self.sv("eps")[0:2, :]),
              waits=[em.now("v")], inc="s")
            E("vector", lambda v: v.reciprocal(rr, t3), waits=[em.now("s")],
              inc="v")
            E("vector", lambda v: v.tensor_mul(rr, rr, rs), inc="v")
        segs = [(mu, pb, 0), (rs, pb, ncols)]
        if dst_xn2 is not None:
            segs.append((rr, pa, 0))
        lo = self.lnlo if hl is self.lnhl else self.lnhle2
        prev_mmb = None
        for i, (src, dstp, off) in enumerate(segs):
            def vhl(v, src=src):
                v.tensor_copy(hl[:, :ncols], src)
                v.tensor_sub(t3, src, hl[:, :ncols])
                return v.tensor_copy(lo[:, :ncols], t3)
            E("vector", vhl,
              waits=[bank_r[sB] if (dstp is pb and i == 0) else None,
                     prev_mmb], inc="v")
            hl_ev = em.now("v")

            def mmb(t, dstp=dstp, off=off):
                t.matmul(dstp[:, off:off + ncols], self.ones2[0:1, :],
                         hl[0:1, :ncols], start=True, stop=False)
                return t.matmul(dstp[:, off:off + ncols], self.ones2[0:1, :],
                                lo[0:1, :ncols], start=False, stop=True)
            E("tensor", mmb,
              waits=[hl_ev, bank_r[sB] if dstp is pb else bank_r[sA]], inc="t")
            prev_mmb = em.now("t")
            if dstp is pb:
                bank_w[sB] = em.now("t")
            else:
                bank_w[sA] = em.now("t")

        def vapply(v):
            last = None
            for kc in range(KD):
                xs = xsrc[:, kc * ncols:(kc + 1) * ncols]
                t0 = self.convt[0][:, :ncols]
                t1 = self.convt[1][:, :ncols]
                v.tensor_sub(t0, xs, pb[:, 0:ncols])
                if dst_xn is not None:
                    last = v.tensor_mul(dst_xn[:, kc * ncols:(kc + 1) * ncols],
                                        t0, pb[:, ncols:2 * ncols])
                if dst_xn2 is not None:
                    v.tensor_mul(t1, t0, pa[:, 0:ncols])
                    last = v.tensor_scalar(
                        dst_xn2[:, kc * ncols:(kc + 1) * ncols], t1,
                        self.sv("ln_g")[:, kc:kc + 1],
                        self.sv("ln_b")[:, kc:kc + 1], ALU.mult, ALU.add)
                if do_sum:
                    if dst_xn is not None:
                        xn_src = dst_xn[:, kc * ncols:(kc + 1) * ncols]
                    else:
                        v.tensor_mul(t1, t0, pb[:, ncols:2 * ncols])
                        xn_src = t1
                    v.reduce_sum(self.red1[:, :], xn_src, AX.X)
                    last = v.tensor_add(self.xnsum[:, kc:kc + 1],
                                        self.xnsum[:, kc:kc + 1],
                                        self.red1[:, :])
            return last
        if "lnrows" in self.dbg:
            def vrows(v):
                v.tensor_copy(self.px2[0:2, 0:ncols], mu)
                v.tensor_copy(self.px2[0:2, TC:TC + ncols], rs)
                return v.tensor_copy(self.px2[0:2, 2 * TC:2 * TC + ncols],
                                     rr if dst_xn2 is not None else var)
            E("vector", vrows, waits=[self.free.get("tap")], inc="v")
            e = self.dma("gpsimd", self.dbg["lnrows"][:, :],
                         self.px2[0:2, 0:3 * TC], waits=[em.now("v")],
                         sem="sp_dbg")
            self.free["tap"] = e
            del self.dbg["lnrows"]
        E("vector", vapply,
          waits=[bank_w[sA], bank_w[sB], free.get("convt"),
                 free.get("ln_dst"), self.ev["svr"]], inc="v")
        bank_r[sA] = em.now("v")
        bank_r[sB] = em.now("v")
        free["convt"] = em.now("v")
        return em.now("v")

    def softplus(self, pbank, dst, biascol):
        E, em, free = self.E, self.em, self.free
        pp = self.psum[pbank]

        def s1(s):
            s.activation(self.sp1[:, :], pp[:, :TC], AF.Abs, bias=biascol)
            s.activation(self.sp1[:, :], self.sp1[:, :], AF.Exp, scale=-1.0)
            return s.activation(self.sp2[:, :], self.sp1[:, :], AF.Ln, bias=self.sv("one"))
        E("scalar", s1, waits=[self.bank_w[pbank], free.get("sp")], inc="s")
        s_ev = em.now("s")

        def v1(v):
            v.tensor_scalar(self.sp1[:, :], pp[:, :TC], biascol, None, ALU.add)
            v.tensor_scalar(self.sp1[:, :], self.sp1[:, :], 0.0, None, ALU.max)
            return v.tensor_add(dst, self.sp1[:, :], self.sp2[:, :])
        E("vector", v1, waits=[s_ev], inc="v")
        self.bank_r[pbank] = em.now("v")
        free["sp"] = em.now("v")
        return em.now("v")

    def emit_main(self, d):
        E, dma, em, ev, free = self.E, self.dma, self.em, self.ev, self.free
        psum, bank_w, bank_r = self.psum, self.bank_w, self.bank_r
        x3 = self.dram[f"x_{d}"].rearrange("p (kc t) -> p kc t", kc=KD)
        inw = self.dram[f"in_wT_{d}"].rearrange("p (kc m) -> p kc m", kc=KD)
        outw = self.dram[f"out_wT_{d}"].rearrange("p (kc m) -> p kc m", kc=NCT)
        ln_e_ev = None

        for ch in range(_TRUNC or NCH):
            exld = dma("sync", self.xch.rearrange("p (kc t) -> p kc t", kc=KD),
                       x3[:, :, 3 + ch * TC:3 + (ch + 1) * TC],
                       waits=[free.get("xch"), free.get("bca")], sem="ld_x")
            free["ln_dst"] = free.get("xn2bf_cons")
            ln_ev = self.emit_ln(self.xch[:, :], TC, None, self.xn2bf,
                                 self.lnr, self.lnhl, do_sum=(d == "f"),
                                 sA=2, sB=3, xb=self.xbf[:, :], sq=self.sqbf,
                                 src_ev=exld)
            free["xch"] = ln_ev
            if ch == 0:
                self.tap(f"xn2bf_{d}", self.xn2bf[:, :], ln_ev,
                         guard="bca")
                e3 = dma("sync", self.xe3.rearrange("p (kc t) -> p kc t", kc=KD),
                         x3[:, :, 0:3], sem="ld_xe")
                ln_e_ev = self.emit_ln(self.xe3[:, :], 3, None, self.xn2e,
                                       self.lnre, self.lnhle, do_sum=False,
                                       sA=6, sB=7, xb=self.xbfe[:, :],
                                       sq=self.sqbfe[:, :], src_ev=e3)

            for mb in range(16):
                wbuf, wb16 = self.wst32[mb % 2], self.wst16[mb % 2]
                e = dma("sync", wbuf.rearrange("p (kc m) -> p kc m", kc=KD),
                        inw[:, :, mb * 256:(mb + 1) * 256],
                        waits=[free.get(f"wst{mb % 2}")] +
                              ([free.get("bca")] if mb % 2 == 0 else []),
                        sem=f"ld_w{mb % 2}")
                E("vector", lambda v, wb16=wb16, wbuf=wbuf:
                  v.tensor_copy(wb16[:, :], wbuf[:, :]),
                  waits=[e, free.get(f"wst16_{mb % 2}")], inc="v")
                cast_ev = em.now("v")
                free[f"wst{mb % 2}"] = cast_ev
                for sub in range(2):
                    mt = mb * 2 + sub
                    bk = mt % 2
                    is_xc = mt < NCT
                    ct = mt if is_xc else mt - NCT

                    def mmg(t, wb16=wb16, sub=sub, bk=bk, is_xc=is_xc, ch=ch):
                        r = None
                        for kc in range(KD):
                            lh = wb16[:, kc * 256 + sub * 128:
                                      kc * 256 + (sub + 1) * 128]
                            r = t.matmul(psum[bk][:, :TC], lh,
                                         self.xn2bf[:, kc * TC:(kc + 1) * TC],
                                         start=(kc == 0), stop=(kc == KD - 1))
                        if ch == 0 and is_xc:
                            for kc in range(KD):
                                lh = wb16[:, kc * 256 + sub * 128:
                                          kc * 256 + (sub + 1) * 128]
                                r = t.matmul(psum[6][:, 0:3], lh,
                                             self.xn2e[:, kc * 3:(kc + 1) * 3],
                                             start=(kc == 0), stop=(kc == KD - 1))
                        return r
                    E("tensor", mmg,
                      waits=[cast_ev, ln_ev, bank_r[bk]] +
                            ([ln_e_ev, bank_r[6]] if (ch == 0 and is_xc) else []),
                      inc="t")
                    bank_w[bk] = em.now("t")
                    free["xn2bf_cons"] = bank_w[bk]
                    free[f"wst16_{mb % 2}"] = bank_w[bk]
                    if ch == 0 and is_xc:
                        bank_w[6] = em.now("t")

                    if is_xc:
                        xr = self.xcpre[ct % 2]
                        E("scalar", lambda s, xr=xr, bk=bk:
                          s.activation(xr[:, 3:3 + TC], psum[bk][:, :TC],
                                       AF.Copy),
                          waits=[bank_w[bk], free.get(f"xcpre{ct % 2}")],
                          inc="s")
                        bank_r[bk] = em.now("s")
                        ev_ev = em.now("s")
                        if ch == 0:
                            E("vector", lambda v, xr=xr, d=d:
                              v.tensor_scalar(xr[:, 0:3], psum[6][:, 0:3],
                                              self.sv(f"bmask_{d}")[:, 0:1],
                                              None, ALU.mult),
                              waits=[bank_w[6], ev_ev, ev["svr"]], inc="v")
                            bank_r[6] = em.now("v")
                        else:
                            E("vector", lambda v, xr=xr, ct=ct, d=d:
                              v.tensor_copy(xr[:, 0:3],
                                            self.bound[d][:, ct * 4:ct * 4 + 3]),
                              waits=[ev_ev], inc="v")

                        def vconv(v, xr=xr, ct=ct, d=d):
                            w0 = self.convw[d][:, ct * DC:ct * DC + 1]
                            cb = self.sv(f"conv_b_{d}")[:, ct:ct + 1]
                            v.tensor_scalar(self.convt[0][:, :], xr[:, 0:TC],
                                            w0, cb, ALU.mult, ALU.add)
                            for j in (1, 2, 3):
                                wj = self.convw[d][:, ct * DC + j:
                                                   ct * DC + j + 1]
                                v.scalar_tensor_tensor(
                                    self.convt[j % 2][:, :], xr[:, j:j + TC],
                                    wj, self.convt[(j + 1) % 2][:, :],
                                    ALU.mult, ALU.add)
                            return v.tensor_copy(
                                self.bound[d][:, ct * 4:ct * 4 + 3],
                                xr[:, TC:TC + 3])
                        E("vector", vconv, waits=[free.get("convt")], inc="v")
                        conv_ev = em.now("v")
                        free["convt"] = conv_ev
                        free[f"xcpre{ct % 2}"] = conv_ev
                        xct_ = self.xcring[ct % 2]
                        E("scalar", lambda s, xct_=xct_:
                          s.activation(xct_[:, :], self.convt[1][:, :],
                                       AF.Silu),
                          waits=[conv_ev, free.get(f"xcr{ct % 2}")], inc="s")
                        silu_ev = em.now("s")
                        free["convt"] = silu_ev
                        E("vector", lambda v, xct_=xct_, ct=ct:
                          v.tensor_copy(self.xcbf[:, ct * TC:(ct + 1) * TC],
                                        xct_[:, :]),
                          waits=[silu_ev, free.get("xcbf_cons"),
                                 free.get("bca")], inc="v")
                        ev[f"xcbf{ct}"] = em.now("v")
                        if ch == 0 and ct == 0:
                            self.tap(f"xc0_{d}", xct_[:, :], silu_ev,
                                     guard=f"xcr{ct % 2}")
                        e = dma("gpsimd",
                                self.dram[f"s_xc_{d}"][:, ct * TOK + ch * TC:
                                                       ct * TOK + (ch + 1) * TC],
                                xct_[:, :], waits=[silu_ev],
                                sem=f"sp_xc{ct % 2}")
                        free[f"xcr{ct % 2}"] = e
                    else:
                        szt = self.szring[ct % 2]
                        E("scalar", lambda s, szt=szt, bk=bk:
                          s.activation(szt[:, :], psum[bk][:, :TC], AF.Silu),
                          waits=[bank_w[bk], free.get(f"szr{ct % 2}")],
                          inc="s")
                        bank_r[bk] = em.now("s")
                        e = dma("gpsimd",
                                self.dram[f"s_sz_{d}"][:, ct * TOK + ch * TC:
                                                       ct * TOK + (ch + 1) * TC],
                                szt[:, :], waits=[em.now("s")],
                                sem=f"sp_sz{ct % 2}")
                        free[f"szr{ct % 2}"] = e

            def mmxp(t, d=d):
                r = None
                for kc in range(NCT):
                    r = t.matmul(psum[4][0:DTR, :TC],
                                 self.xpw[d][:, kc * 96:kc * 96 + 64],
                                 self.xcbf[:, kc * TC:(kc + 1) * TC],
                                 start=(kc == 0), stop=(kc == NCT - 1))
                for kc in range(NCT):
                    r = t.matmul(psum[5][0:16, :TC],
                                 self.xpw[d][:, kc * 96 + 64:kc * 96 + 80],
                                 self.xcbf[:, kc * TC:(kc + 1) * TC],
                                 start=(kc == 0), stop=(kc == NCT - 1))
                for kc in range(NCT):
                    r = t.matmul(psum[6][0:16, :TC],
                                 self.xpw[d][:, kc * 96 + 80:kc * 96 + 96],
                                 self.xcbf[:, kc * TC:(kc + 1) * TC],
                                 start=(kc == 0), stop=(kc == NCT - 1))
                return r
            E("tensor", mmxp,
              waits=[ev[f"xcbf{c}"] for c in range(NCT)] +
                    [bank_r[4], bank_r[5], bank_r[6]], inc="t")
            bank_w[4] = bank_w[5] = bank_w[6] = em.now("t")
            free["xcbf_cons"] = bank_w[4]

            E("scalar", lambda s: s.activation(self.dtbf[:, :],
                                               psum[4][0:DTR, :TC], AF.Copy),
              waits=[bank_w[4], free.get("dtbf"), ev.get("dtspill")], inc="s")
            bank_r[4] = em.now("s")
            dt_ev = em.now("s")
            if ch == 0:
                self.tap(f"dt0_{d}", self.dtbf[:, :], dt_ev, guard="dtbf")
            ev["dtspill"] = dma("gpsimd",
                self.dram[f"s_dt_{d}"][:, ch * TC:(ch + 1) * TC],
                self.dtbf[:, :], waits=[dt_ev], sem="sp_dt")

            def vbc(v):
                v.tensor_copy(self.bhi[:, :], psum[5][0:16, :TC])
                v.tensor_sub(self.bclo[:, :], psum[5][0:16, :TC],
                             self.bhi[:, :])
                v.tensor_copy(self.blo[:, :], self.bclo[:, :])
                v.tensor_copy(self.chi[:, :], psum[6][0:16, :TC])
                v.tensor_sub(self.bclo[:, :], psum[6][0:16, :TC],
                             self.chi[:, :])
                return v.tensor_copy(self.clo[:, :], self.bclo[:, :])
            E("vector", vbc,
              waits=[bank_w[5], bank_w[6], free.get("bchl"),
                     em.now("sp_bc") if "sp_bc" in em.count else None],
              inc="v")
            hl_ev = em.now("v")
            bank_r[5] = hl_ev
            bank_r[6] = hl_ev
            ebc1 = dma("gpsimd",
                self.dram[f"s_bc_{d}"][:, 2 * ch * TC:(2 * ch + 1) * TC],
                self.chi[:, :], waits=[hl_ev], sem="sp_bc")
            dma("gpsimd",
                self.dram[f"s_bc_{d}"][:, (2 * ch + 1) * TC:(2 * ch + 2) * TC],
                self.clo[:, :], waits=[hl_ev, ebc1], sem="sp_bc")

            for grp, (shi, slo) in enumerate(((self.bhi, self.blo),
                                              (self.chi, self.clo))):
                for s_ in range(DS):
                    seg = self.bca[:, (grp * DS + s_) * TC:
                                   (grp * DS + s_ + 1) * TC]
                    half = (s_ % 2) * TC

                    def mmbc(t, s_=s_, shi=shi, slo=slo, half=half):
                        t.matmul(psum[5][:, half:half + TC],
                                 self.selw[0:16, s_ * 128:(s_ + 1) * 128],
                                 shi[:, :], start=True, stop=False)
                        return t.matmul(psum[5][:, half:half + TC],
                                        self.selw[0:16, s_ * 128:(s_ + 1) * 128],
                                        slo[:, :], start=False, stop=True)
                    E("tensor", mmbc,
                      waits=[hl_ev, bank_r[5],
                             free.get("bca") if (grp == 0 and s_ == 0)
                             else None], inc="t")
                    bank_w[5] = em.now("t")
                    E("scalar", lambda s, seg=seg, half=half:
                      s.activation(seg, psum[5][:, half:half + TC], AF.Copy),
                      waits=[bank_w[5]], inc="s")
                    bank_r[5] = em.now("s")
            bca_ev = em.now("s")
            if ch == 0:
                self.tap(f"bca_{d}", self.bca[:, :], bca_ev, guard="bca")

            for ct in range(NCT):
                dl = self.dlt[ct % 2]

                def mmdtp(t, ct=ct, d=d):
                    return t.matmul(psum[4][:, :TC],
                                    self.dtpw[d][:, ct * 128:(ct + 1) * 128],
                                    self.dtbf[:, :], start=True, stop=True)
                E("tensor", mmdtp, waits=[dt_ev, bank_r[4],
                                          free.get(f"dlt{ct % 2}")], inc="t")
                bank_w[4] = em.now("t")
                sp_ev = self.softplus(4, dl[:, :],
                                      self.sv(f"dtp_b_{d}")[:, ct:ct + 1])

                def vsd(v, dl=dl, ct=ct, d=d):
                    v.reduce_sum(self.red1[:, :], dl[:, :], AX.X)
                    return v.tensor_add(self.sumdl[d][:, ct:ct + 1],
                                        self.sumdl[d][:, ct:ct + 1],
                                        self.red1[:, :])
                E("vector", vsd, inc="v")
                er = dma("sync", self.xcrel[ct % 2][:, :],
                         self.dram[f"s_xc_{d}"][:, ct * TOK + ch * TC:
                                                ct * TOK + (ch + 1) * TC],
                         waits=[free.get(f"xcl{ct % 2}"),
                                em.now(f"sp_xc{ct % 2}")],
                         sem=f"ld_xc{ct % 2}")
                esz = dma("sync", self.szrel[ct % 2][:, :],
                          self.dram[f"s_sz_{d}"][:, ct * TOK + ch * TC:
                                                 ct * TOK + (ch + 1) * TC],
                          waits=[free.get(f"szl{ct % 2}"),
                                 em.now(f"sp_sz{ct % 2}")],
                          sem=f"ld_sz{ct % 2}")
                E("vector", lambda v, ct=ct, dl=dl:
                  v.tensor_mul(self.dut[:, :], dl[:, :],
                               self.xcrel[ct % 2][:, :]),
                  waits=[er, free.get("dut")], inc="v")
                du_ev = em.now("v")
                if ch == 0 and ct == 0:
                    self.tap(f"dlt0_{d}", dl[:, :], du_ev, guard=f"dlt{ct % 2}")
                    self.tap(f"dut0_{d}", self.dut[:, :], du_ev, guard="dut")

                for sg in range(NSG):
                    ara = self.ar_a2[sg % 2]

                    def sda(s, ct=ct, sg=sg, dl=dl, d=d, ara=ara):
                        r = None
                        for sl_ in range(SG):
                            s_ = sg * SG + sl_
                            r = s.activation(
                                ara[:, sl_ * TC:(sl_ + 1) * TC],
                                dl[:, :], AF.Exp,
                                scale=self.Asb[d][:, ct * DS + s_:
                                                  ct * DS + s_ + 1])
                        return r
                    E("scalar", sda, waits=[sp_ev, free.get(f"ar_a{sg % 2}"),
                                            ev[f"A_{d}"]], inc="s")
                    da_ev = em.now("s")

                    def vb(v, ct=ct, sg=sg, d=d, ara=ara):
                        for sl_ in range(SG):
                            s_ = sg * SG + sl_
                            v.tensor_mul(self.ar_b[:, sl_ * TC:(sl_ + 1) * TC],
                                         self.dut[:, :],
                                         self.bca[:, s_ * TC:(s_ + 1) * TC])
                        a3 = ara.rearrange("p (s t) -> p s t", s=SG)
                        b3 = self.ar_b.rearrange("p (s t) -> p s t", s=SG)
                        hs = self.hstate[d][:, ct * DS + sg * SG:
                                            ct * DS + (sg + 1) * SG]
                        v.tensor_mul(self.mtt[0][:, 0:SG], a3[:, :, 0], hs)
                        v.tensor_add(b3[:, :, 0], b3[:, :, 0],
                                     self.mtt[0][:, 0:SG])
                        return v.memset(a3[:, :, 0:1], 0.0)
                    E("vector", vb, waits=[da_ev, du_ev, bca_ev,
                                           free.get("ar_b")], inc="v")

                    E("vector", lambda v, ara=ara: v.tensor_tensor_scan(
                        self.ar_h[:, :], ara[:, :], self.ar_b[:, :],
                        0.0, ALU.mult, ALU.add),
                      waits=[free.get("ar_h")], inc="v")
                    scan_ev = em.now("v")
                    if ch == 0 and ct == 0 and sg == 0:
                        self.tap(f"arh0_{d}", self.ar_h[:, :], scan_ev,
                                 guard="ar_h")
                    free[f"ar_a{sg % 2}"] = scan_ev
                    free["ar_b"] = scan_ev

                    E("vector", lambda v, ct=ct, sg=sg, d=d: v.tensor_copy(
                        self.hstate[d][:, ct * DS + sg * SG:
                                       ct * DS + (sg + 1) * SG],
                        self.ar_h.rearrange("p (s t) -> p s t",
                                            s=SG)[:, :, TC - 1]),
                      inc="v")

                    def vy(v, sg=sg):
                        v.tensor_mul(self.ar_m[:, :], self.ar_h[:, :],
                                     self.bca[:, (DS + sg * SG) * TC:
                                              (DS + (sg + 1) * SG) * TC])
                        v.tensor_add(self.fold[:, :],
                                     self.ar_m[:, 0:2 * TC],
                                     self.ar_m[:, 2 * TC:4 * TC])
                        if sg == 0:
                            return v.tensor_add(self.yacc[0][:, :],
                                                self.fold[:, 0:TC],
                                                self.fold[:, TC:2 * TC])
                        v.tensor_add(self.mtt[1][:, :], self.fold[:, 0:TC],
                                     self.fold[:, TC:2 * TC])
                        return v.tensor_add(self.yacc[sg % 2][:, :],
                                            self.yacc[(sg + 1) % 2][:, :],
                                            self.mtt[1][:, :])
                    E("vector", vy, inc="v")
                    free["ar_h"] = em.now("v")
                free["bca"] = em.now("v")
                free["dut"] = em.now("v")
                free[f"dlt{ct % 2}"] = em.now("v")

                yfin = self.yacc[(NSG - 1) % 2]

                def vgate(v, ct=ct, d=d, yfin=yfin):
                    v.scalar_tensor_tensor(self.mtt[0][:, :],
                                           self.xcrel[ct % 2][:, :],
                                           self.sv(f"Dk_{d}")[:, ct:ct + 1],
                                           yfin[:, :], ALU.mult, ALU.add)
                    return v.tensor_mul(self.g1ch[:, ct * TC:(ct + 1) * TC],
                                        self.mtt[0][:, :],
                                        self.szrel[ct % 2][:, :])
                E("vector", vgate, waits=[esz, free.get("g1_cons")], inc="v")
                gate_ev = em.now("v")
                if ch == 0 and ct == 0:
                    self.tap(f"y0_{d}", yfin[:, :], gate_ev, guard="yacc")
                free[f"xcl{ct % 2}"] = gate_ev
                free[f"szl{ct % 2}"] = gate_ev
                ev[f"g1_{ct}"] = gate_ev

            for mt in range(KD):
                wbuf, wb16 = self.wst32[mt % 2], self.wst16[mt % 2]
                e = dma("sync",
                        wbuf[:, :NCT * 128].rearrange("p (kc m) -> p kc m",
                                                      kc=NCT),
                        outw[:, :, mt * 128:(mt + 1) * 128],
                        waits=[free.get(f"wst{mt % 2}")] +
                              ([free.get("bca")] if mt % 2 == 0 else []),
                        sem=f"ld_w{mt % 2}")
                E("vector", lambda v, wb16=wb16, wbuf=wbuf:
                  v.tensor_copy(wb16[:, :NCT * 128], wbuf[:, :NCT * 128]),
                  waits=[e, free.get(f"wst16_{mt % 2}")], inc="v")
                cast_ev = em.now("v")
                free[f"wst{mt % 2}"] = cast_ev
                bk = mt % 2

                def mmop(t, wb16=wb16, bk=bk):
                    r = None
                    for kc in range(NCT):
                        r = t.matmul(psum[bk][:, :TC],
                                     wb16[:, kc * 128:(kc + 1) * 128],
                                     self.g1ch[:, kc * TC:(kc + 1) * TC],
                                     start=(kc == 0), stop=(kc == NCT - 1))
                    return r
                E("tensor", mmop,
                  waits=[cast_ev, bank_r[bk]] +
                        [ev[f"g1_{c}"] for c in range(NCT)], inc="t")
                bank_w[bk] = em.now("t")
                free["g1_cons"] = bank_w[bk]
                free[f"wst16_{mt % 2}"] = bank_w[bk]
                yp = self.ypst[mt % 2]
                E("scalar", lambda s, yp=yp, bk=bk:
                  s.activation(yp[:, :], psum[bk][:, :TC], AF.Copy),
                  waits=[bank_w[bk], free.get(f"ypst{mt % 2}")], inc="s")
                bank_r[bk] = em.now("s")
                e = dma("gpsimd",
                        self.dram[f"s_yp_{d}"][:, mt * TOK + ch * TC:
                                               mt * TOK + (ch + 1) * TC],
                        yp[:, :], waits=[em.now("s")],
                        sem=f"sp_yp{mt % 2}")
                free[f"ypst{mt % 2}"] = e
            if ch == 0:
                self.tap(f"ypch0_{d}", self.ypst[(KD - 1) % 2][:, :],
                         em.now("s"), guard=f"ypst{(KD - 1) % 2}")
            free["bchl"] = em.now("s")
            free["dtbf"] = em.now("v")

        if _TRUNC:
            return

        def vtk(v, d=d):
            last = None
            for ct in range(NCT):
                last = v.tensor_scalar(self.tkt[:, ct * DS:(ct + 1) * DS],
                                       self.Asb[d][:, ct * DS:(ct + 1) * DS],
                                       self.sumdl[d][:, ct:ct + 1], None,
                                       ALU.mult)
            return last
        E("vector", vtk, inc="v")
        E("scalar", lambda s: s.activation(self.tkt[:, :], self.tkt[:, :],
                                           AF.Exp),
          waits=[em.now("v")], inc="s")
        tk_ev = em.now("s")

        def vstage(v, d=d):
            v.tensor_copy(self.cctmp[:, 0:NCT * DS], self.tkt[:, :])
            return v.tensor_copy(self.cctmp[:, NCT * DS:], self.hstate[d][:, :])
        E("vector", vstage, waits=[tk_ev, free.get("cctmp")], inc="v")
        e = dma("gpsimd", self.dram[f"cc_in_{d}"][:, :], self.cctmp[:, :],
                waits=[em.now("v")], sem="sp_cc")
        ev[f"ccin_{d}"] = e
        free["cctmp"] = e
        E("gpsimd", lambda g, d=d: g.collective_compute(
            "AllGather", ALU.bypass,
            replica_groups=[[0, 1, 2, 3], [4, 5, 6, 7]],
            ins=[self.dram[f"cc_in_{d}"][:, :].opt()],
            outs=[self.dram[f"cc_out_{d}"][:, :].opt()]),
          waits=[e], inc="cc")
        ev[f"cc_{d}"] = em.now("cc")

    def emit_xnsum_cc(self):
        E, dma, em, ev, free = self.E, self.dma, self.em, self.ev, self.free
        E("vector", lambda v: v.tensor_copy(self.cctmp[:, 0:KD],
                                            self.xnsum[:, :]),
          waits=[free.get("cctmp")], inc="v")
        e = dma("gpsimd", self.dram["cc_in_m"][:, :], self.cctmp[:, 0:KD],
                waits=[em.now("v")], sem="sp_cc")
        free["cctmp"] = e
        E("gpsimd", lambda g: g.collective_compute(
            "AllReduce", ALU.add, replica_groups=[[0, 1, 2, 3], [4, 5, 6, 7]],
            ins=[self.dram["cc_in_m"][:, :].opt()],
            outs=[self.dram["cc_out_m"][:, :].opt()]),
          waits=[e], inc="cc")
        ev["cc_m"] = em.now("cc")

    def emit_ffn(self):
        E, dma, em, ev, free = self.E, self.dma, self.em, self.ev, self.free
        psum, bank_w, bank_r = self.psum, self.bank_w, self.bank_r
        e = dma("sync", self.cctmp[:, 0:KD], self.dram["cc_out_m"][:, :],
                waits=[ev["cc_m"], free.get("cctmp")], sem="ld_cc")
        E("vector", lambda v: v.tensor_scalar(self.cctmp[:, KD:2 * KD],
                                              self.cctmp[:, 0:KD], 1.0 / L,
                                              None, ALU.mult),
          waits=[e], inc="v")
        E("vector", lambda v: v.tensor_copy(self.pb1[:, 0:KD],
                                            self.cctmp[:, KD:2 * KD]),
          waits=[free.get("pb1")], inc="v")
        mean_ev = em.now("v")
        free["cctmp"] = mean_ev

        eg1 = dma("sync", self.wstB[:, :KD * 256],
                  self.dram["gc_w1"].rearrange("p (kc m) -> p kc m", kc=KD)
                  [:, :, 0:256], waits=[free.get("wst1")], sem="ld_w1")
        eg1b = dma("sync", self.wstA[:, :KD * 256],
                   self.dram["gc_w1"].rearrange("p (kc m) -> p kc m", kc=KD)
                   [:, :, 256:512],
                   waits=[free.get("wst0"), free.get("bca")], sem="ld_w0")
        E("vector", lambda v: v.tensor_copy(self.wst16[0][:, :],
                                            self.wstB[:, :]),
          waits=[eg1, free.get("wst16_0")], inc="v")
        free["wst1"] = em.now("v")
        E("vector", lambda v: v.tensor_copy(self.wst16[1][:, :],
                                            self.wstA[:, :]),
          waits=[eg1b, free.get("wst16_1")], inc="v")
        free["wst0"] = em.now("v")
        g1c = em.now("v")

        def mmgf1(t):
            r = None
            for mt in range(4):
                wb = self.wst16[0] if mt < 2 else self.wst16[1]
                mo = mt % 2
                for kc in range(KD):
                    r = t.matmul(psum[2][:, mt:mt + 1],
                                 wb[:, kc * 256 + mo * 128:
                                    kc * 256 + (mo + 1) * 128],
                                 self.pb1[:, kc:kc + 1],
                                 start=(kc == 0), stop=(kc == KD - 1))
            return r
        E("tensor", mmgf1, waits=[g1c, mean_ev, bank_r[2]], inc="t")
        bank_w[2] = em.now("t")
        free["wst16_0"] = bank_w[2]
        free["wst16_1"] = bank_w[2]
        def sgf1(s):
            r = None
            for mt in range(4):
                r = s.activation(self.px1[:, mt:mt + 1], psum[2][:, mt:mt + 1],
                                 AF.Gelu, bias=self.sv("gc_b1")[:, mt:mt + 1])
            return r
        E("scalar", sgf1, waits=[bank_w[2], free.get("px1")], inc="s")
        bank_r[2] = em.now("s")
        E("vector", lambda v: v.tensor_copy(self.pb1[:, KD:KD + 4],
                                            self.px1[:, 0:4]),
          waits=[em.now("s")], inc="v")
        h1_ev = em.now("v")
        eg2 = dma("sync", self.wstB[:, :4 * 256], self.dram["gc_w2"][:, :],
                  waits=[free.get("wst1")], sem="ld_w1")
        E("vector", lambda v: v.tensor_copy(self.wst16[0][:, :4 * 256],
                                            self.wstB[:, :4 * 256]),
          waits=[eg2, free.get("wst16_0")], inc="v")
        free["wst1"] = em.now("v")
        g2c = em.now("v")

        def mmgf2(t):
            r = None
            for mt in range(2):
                for kc in range(4):
                    r = t.matmul(psum[3][:, mt:mt + 1],
                                 self.wst16[0][:, kc * 256 + mt * 128:
                                               kc * 256 + (mt + 1) * 128],
                                 self.pb1[:, KD + kc:KD + kc + 1],
                                 start=(kc == 0), stop=(kc == 3))
            return r
        E("tensor", mmgf2, waits=[g2c, h1_ev, bank_r[3]], inc="t")
        bank_w[3] = em.now("t")
        free["wst16_0"] = bank_w[3]
        def sgf2(s):
            r = None
            for mt in range(2):
                r = s.activation(self.px1[:, 4 + mt:5 + mt],
                                 psum[3][:, mt:mt + 1], AF.Gelu,
                                 bias=self.sv("gc_b2")[:, mt:mt + 1])
            return r
        E("scalar", sgf2, waits=[bank_w[3]], inc="s")
        bank_r[3] = em.now("s")
        E("vector", lambda v: v.tensor_copy(self.pb1[:, KD + 4:KD + 6],
                                            self.px1[:, 4:6]),
          waits=[em.now("s")], inc="v")
        h2_ev = em.now("v")
        eg3 = dma("sync", self.wstB[:, :4], self.dram["gc_w3"][:, :],
                  waits=[free.get("wst1")], sem="ld_w1")
        E("vector", lambda v: v.tensor_copy(self.wst16[0][:, :4],
                                            self.wstB[:, :4]),
          waits=[eg3, free.get("wst16_0")], inc="v")
        free["wst1"] = em.now("v")
        g3c = em.now("v")

        def mmgf3(t):
            r = None
            for kc in range(2):
                r = t.matmul(psum[2][0:2, 0:1],
                             self.wst16[0][:, kc * 2:kc * 2 + 2],
                             self.pb1[:, KD + 4 + kc:KD + 5 + kc],
                             start=(kc == 0), stop=(kc == 1))
            return r
        E("tensor", mmgf3, waits=[g3c, h2_ev, bank_r[2]], inc="t")
        bank_w[2] = em.now("t")
        free["wst16_0"] = bank_w[2]
        E("scalar", lambda s: s.activation(self.gfrow[:, 3:4],
                                           psum[2][0:2, 0:1], AF.Sigmoid,
                                           bias=self.sv("gc_b3")[0:2, :]),
          waits=[bank_w[2]], inc="s")
        bank_r[2] = em.now("s")

        def vgf(v):
            v.tensor_scalar(self.gfrow[:, 0:1], self.gfrow[:, 3:4], 0.7, None,
                            ALU.mult)
            v.tensor_scalar(self.gfrow[:, 1:2], self.sv("ar")[0:2, :], 2.0, None,
                            ALU.mult)
            return v.tensor_scalar(self.gfrow[:, 2:3], self.sv("ar")[0:2, :], -1.0,
                                   0.1, ALU.mult, ALU.add)
        E("vector", vgf, waits=[em.now("s")], inc="v")
        gf_ev = em.now("v")

        cn1bf = self.g1ch[:, 0:KD * D // 2]
        cn1bf2 = self.bca[:, 0:KD * D // 2]
        cn2bf = self.pb1[:, 0:KD * 512 // 2]
        cn3bf = self.xcbf[:, 0:4 * 256]
        cn4bf = self.xcbf[:, 4 * 256:4 * 256 + 4]
        e3 = dma("sync", self.wstB[:, :4 * 256], self.dram["cn_w3"][:, :],
                 waits=[free.get("wst1")], sem="ld_w1")
        E("vector", lambda v: v.tensor_copy(cn3bf, self.wstB[:, :4 * 256]),
          waits=[e3, free.get("xcbf_cons"), free.get("stats_sq")], inc="v")
        free["wst1"] = em.now("v")
        e4 = dma("sync", self.wstB[:, 4 * 256:4 * 256 + 4],
                 self.dram["cn_w4"][:, :], waits=[e3], sem="ld_w1")
        E("vector", lambda v: v.tensor_copy(cn4bf,
                                            self.wstB[:, 4 * 256:4 * 256 + 4]),
          waits=[e4], inc="v")
        cn_ev = em.now("v")
        free["wst1"] = cn_ev
        free["stats_sq"] = None

        x3 = self.dram["x_f"].rearrange("p (kc t) -> p kc t", kc=KD)
        cn1w = self.dram["cn_w1"].rearrange("p (kc m) -> p kc m", kc=KD)
        cn2w = self.dram["cn_w2"].rearrange("p (kc m) -> p kc m", kc=KD)
        outc = self.px1
        h1f = self.px2
        h1b = self.xn2bf
        outb = self.xbf
        for ch in range(NCH):
            exld = dma("sync", self.xch.rearrange("p (kc t) -> p kc t", kc=KD),
                       x3[:, :, 3 + ch * TC:3 + (ch + 1) * TC],
                       waits=[free.get("xch")], sem="ld_x")
            free["ln_dst"] = free.get("outc_cons")
            free["stats_xb"] = free.get("outb_cons")
            ln_ev = self.emit_ln(self.xch[:, :], TC, outc[:, :], None,
                                 self.lnr, self.lnhl, do_sum=False, sA=2,
                                 sB=3, xb=self.xbf[:, :], sq=self.ffnsq,
                                 src_ev=exld)
            free["xch"] = ln_ev
            prev_ev = ln_ev
            for cy in range(3):
                E("vector", lambda v: v.tensor_copy(outb[:, :], outc[:, :]),
                  waits=[prev_ev, free.get("outb_cons")], inc="v")
                ob_ev = em.now("v")
                for mbk in range(4):
                    wbuf = self.wst32[mbk % 2]
                    wb16 = self.wst16[mbk % 2]
                    e = dma("sync", wbuf.rearrange("p (kc m) -> p kc m", kc=KD),
                            cn1w[:, :, mbk * 256:(mbk + 1) * 256],
                            waits=[free.get(f"wst{mbk % 2}")],
                            sem=f"ld_w{mbk % 2}")
                    E("vector", lambda v, wb16=wb16, wbuf=wbuf:
                      v.tensor_copy(wb16[:, :], wbuf[:, :]),
                      waits=[e, free.get(f"wst16_{mbk % 2}")], inc="v")
                    cast_ev = em.now("v")
                    free[f"wst{mbk % 2}"] = cast_ev
                    for sub in range(2):
                        mt = mbk * 2 + sub
                        bk = mt % 2

                        def mmh1(t, wb16=wb16, sub=sub, bk=bk):
                            r = None
                            for kc in range(KD):
                                lh = wb16[:, kc * 256 + sub * 128:
                                          kc * 256 + (sub + 1) * 128]
                                r = t.matmul(psum[bk][:, :TC], lh,
                                             outb[:, kc * TC:(kc + 1) * TC],
                                             start=(kc == 0),
                                             stop=(kc == KD - 1))
                            return r
                        E("tensor", mmh1, waits=[ob_ev, cast_ev, bank_r[bk]],
                          inc="t")
                        bank_w[bk] = em.now("t")
                        free[f"wst16_{mbk % 2}"] = bank_w[bk]
                        free["outb_cons"] = bank_w[bk]
                        E("scalar", lambda s, mt=mt, bk=bk:
                          s.activation(h1f[:, mt * TC:(mt + 1) * TC],
                                       psum[bk][:, :TC], AF.Gelu,
                                       bias=self.sv("cn_b1")[:, mt:mt + 1]),
                          waits=[bank_w[bk], free.get("h1f")], inc="s")
                        bank_r[bk] = em.now("s")
                E("vector", lambda v: v.tensor_copy(h1b[:, :], h1f[:, :]),
                  waits=[em.now("s"), free.get("h1b_cons")], inc="v")
                h1b_ev = em.now("v")
                for mbk in range(2):
                    wbuf = self.wst32[mbk % 2]
                    wb16 = self.wst16[mbk % 2]
                    e = dma("sync", wbuf.rearrange("p (kc m) -> p kc m", kc=KD),
                            cn2w[:, :, mbk * 256:(mbk + 1) * 256],
                            waits=[free.get(f"wst{mbk % 2}")],
                            sem=f"ld_w{mbk % 2}")
                    E("vector", lambda v, wb16=wb16, wbuf=wbuf:
                      v.tensor_copy(wb16[:, :], wbuf[:, :]),
                      waits=[e, free.get(f"wst16_{mbk % 2}")], inc="v")
                    cast_ev = em.now("v")
                    free[f"wst{mbk % 2}"] = cast_ev
                    for sub in range(2):
                        mt = mbk * 2 + sub
                        bk = mt % 2

                        def mmh2(t, wb16=wb16, sub=sub, bk=bk):
                            r = None
                            for kc in range(KD):
                                lh = wb16[:, kc * 256 + sub * 128:
                                          kc * 256 + (sub + 1) * 128]
                                r = t.matmul(psum[bk][:, :TC], lh,
                                             h1b[:, kc * TC:(kc + 1) * TC],
                                             start=(kc == 0),
                                             stop=(kc == KD - 1))
                            return r
                        E("tensor", mmh2, waits=[h1b_ev, cast_ev, bank_r[bk]],
                          inc="t")
                        bank_w[bk] = em.now("t")
                        free[f"wst16_{mbk % 2}"] = bank_w[bk]
                        free["h1b_cons"] = bank_w[bk]
                        E("scalar", lambda s, mt=mt, bk=bk:
                          s.activation(h1f[:, mt * TC:(mt + 1) * TC],
                                       psum[bk][:, :TC], AF.Gelu,
                                       bias=self.sv("cn_b2")[:, mt:mt + 1]),
                          waits=[bank_w[bk]], inc="s")
                        bank_r[bk] = em.now("s")
                free["h1f"] = em.now("s")
                E("vector", lambda v: v.tensor_copy(h1b[:, 0:4 * TC],
                                                    h1f[:, 0:4 * TC]),
                  waits=[em.now("s")], inc="v")
                h2b_ev = em.now("v")
                free["h1b_cons"] = None
                for mt in range(2):
                    bk = mt % 2

                    def mmh3(t, mt=mt, bk=bk):
                        r = None
                        for kc in range(4):
                            r = t.matmul(psum[bk][:, :TC],
                                         cn3bf[:, kc * 256 + mt * 128:
                                               kc * 256 + (mt + 1) * 128],
                                         h1b[:, kc * TC:(kc + 1) * TC],
                                         start=(kc == 0), stop=(kc == 3))
                        return r
                    E("tensor", mmh3, waits=[h2b_ev, cn_ev, bank_r[bk]],
                      inc="t")
                    bank_w[bk] = em.now("t")
                    free["h1b_cons"] = bank_w[bk]
                    E("scalar", lambda s, mt=mt, bk=bk:
                      s.activation(h1f[:, mt * TC:(mt + 1) * TC],
                                   psum[bk][:, :TC], AF.Gelu,
                                   bias=self.sv("cn_b3")[:, mt:mt + 1]),
                      waits=[bank_w[bk], free.get("h1f")], inc="s")
                    bank_r[bk] = em.now("s")
                free["h1f"] = em.now("s")
                E("vector", lambda v: v.tensor_copy(h1b[:, 0:2 * TC],
                                                    h1f[:, 0:2 * TC]),
                  waits=[em.now("s"), free.get("h1b_cons")], inc="v")
                h3b_ev = em.now("v")
                free["h1b_cons"] = None

                def mmtf(t):
                    r = None
                    for kc in range(2):
                        r = t.matmul(psum[2][0:2, :TC],
                                     cn4bf[:, kc * 2:kc * 2 + 2],
                                     h1b[:, kc * TC:(kc + 1) * TC],
                                     start=(kc == 0), stop=(kc == 1))
                    return r
                E("tensor", mmtf, waits=[h3b_ev, cn_ev, bank_r[2]], inc="t")
                bank_w[2] = em.now("t")
                free["h1b_cons"] = bank_w[2]
                E("scalar", lambda s: s.activation(self.lnr[0][:, :],
                                                   psum[2][0:2, :TC],
                                                   AF.Sigmoid,
                                                   bias=self.sv("cn_b4")[0:2, :]),
                  waits=[bank_w[2]], inc="s")
                bank_r[2] = em.now("s")

                def vcp(v):
                    v.tensor_scalar(self.lnr[1][:, :], self.lnr[0][:, :], 0.3,
                                    self.gfrow[:, 0:1], ALU.mult, ALU.add)
                    v.tensor_scalar(self.lnr[1][:, :], self.lnr[1][:, :],
                                    self.gfrow[:, 1:2], self.gfrow[:, 2:3],
                                    ALU.mult, ALU.add)
                    v.tensor_copy(self.lnhl[:, :], self.lnr[1][:, :])
                    v.tensor_sub(self.lnr[2][:, :], self.lnr[1][:, :],
                                 self.lnhl[:, :])
                    return v.tensor_copy(self.lnlo[:, :], self.lnr[2][:, :])
                E("vector", vcp, waits=[em.now("s"), gf_ev], inc="v")
                def mmcpl(t):
                    t.matmul(psum[3][:, :TC], self.ones2[0:1, :],
                             self.lnhl[0:1, :], start=True, stop=False)
                    return t.matmul(psum[3][:, :TC], self.ones2[0:1, :],
                                    self.lnlo[0:1, :], start=False, stop=True)
                E("tensor", mmcpl, waits=[em.now("v"), bank_r[3]], inc="t")
                bank_w[3] = em.now("t")

                def vtw(v):
                    last = None
                    for qa, qb, sgn in ((0, 2, +1), (1, 3, -1)):
                        for k in range(2):
                            a = outc[:, (2 * qa + k) * TC:
                                     (2 * qa + k + 1) * TC]
                            b = outc[:, (2 * qb + k) * TC:
                                     (2 * qb + k + 1) * TC]
                            ta = self.mtt[0][:, :]
                            tb = self.mtt[1][:, :]
                            v.tensor_mul(ta, psum[3][:, :TC], b)
                            v.tensor_mul(tb, psum[3][:, :TC], a)
                            if sgn > 0:
                                v.tensor_add(a, a, ta)
                                last = v.tensor_sub(b, b, tb)
                            else:
                                v.tensor_sub(a, a, ta)
                                last = v.tensor_add(b, b, tb)
                    return last
                E("vector", vtw, waits=[bank_w[3], free.get("mtt0"),
                                        free.get("mtt1")], inc="v")
                bank_r[3] = em.now("v")
                prev_ev = em.now("v")
            e = dma("gpsimd", self.dram["s_ffn"].rearrange(
                "p (kc t) -> p kc t", kc=KD)[:, :, ch * TC:(ch + 1) * TC],
                outc.rearrange("p (kc t) -> p kc t", kc=KD),
                waits=[prev_ev], sem="sp_ffn")
            free["outc_cons"] = e
        free["px1"] = e
        free["pb1"] = em.now("t")
        free["ypb"] = em.now("t")
        free["stats_xb"] = em.now("t")
        free["outb_cons"] = None
        free["xcbf_cons"] = em.now("t")

    def emit_combine(self, d):
        E, dma, em, ev, free = self.E, self.dma, self.em, self.ev, self.free
        order = (1, 2, 3) if d == "f" else (2, 1, 0)
        first = 0 if d == "f" else 3
        e0 = dma("sync", self.cctmp[:, :],
                 self.dram[f"cc_out_{d}"][128 * first:128 * (first + 1), :],
                 waits=[ev[f"cc_{d}"], free.get("cctmp"),
                        ev.get("ccin_b")], sem="ld_cc")
        E("vector", lambda v: v.tensor_copy(self.hpre[0][:, :],
                                            self.cctmp[:, NCT * DS:]),
          waits=[e0], inc="v")
        oh = self.sv(f"oh_{d}")
        for i, rk in enumerate(order[:2]):
            E("vector", lambda v, i=i, oh=oh, d=d: v.scalar_tensor_tensor(
                self.hin[d][:, :], self.hpre[0][:, :], oh[:, i + 1:i + 2],
                self.hin[d][:, :], ALU.mult, ALU.add), inc="v")
            e = dma("sync", self.cctmp[:, :],
                    self.dram[f"cc_out_{d}"][128 * rk:128 * (rk + 1), :],
                    waits=[em.now("v")], sem="ld_cc")

            def vcomb(v):
                v.tensor_mul(self.hpre[1][:, :], self.cctmp[:, 0:NCT * DS],
                             self.hpre[0][:, :])
                return v.tensor_add(self.hpre[0][:, :], self.hpre[1][:, :],
                                    self.cctmp[:, NCT * DS:])
            E("vector", vcomb, waits=[e], inc="v")
        E("vector", lambda v, oh=oh, d=d: v.scalar_tensor_tensor(
            self.hin[d][:, :], self.hpre[0][:, :], oh[:, 3:4],
            self.hin[d][:, :], ALU.mult, ALU.add), inc="v")
        ev[f"hin_{d}"] = em.now("v")
        free["cctmp"] = em.now("v")
        E("vector", lambda v: v.memset(self.pcarry[:, :], 0.0), inc="v")

    def emit_corr(self, d):
        E, dma, em, ev, free = self.E, self.dma, self.em, self.ev, self.free
        psum, bank_w, bank_r = self.psum, self.bank_w, self.bank_r
        outw = self.dram[f"out_wT_{d}"].rearrange("p (kc m) -> p kc m", kc=NCT)
        for ch in range(NCH):
            edt = dma("sync", self.dtbf[:, :],
                      self.dram[f"s_dt_{d}"][:, ch * TC:(ch + 1) * TC],
                      waits=[free.get("dtbf"), em.now("sp_dt")], sem="ld_dt")
            ech = dma("sync", self.chi[:, :],
                      self.dram[f"s_bc_{d}"][:, 2 * ch * TC:(2 * ch + 1) * TC],
                      waits=[free.get("bchl"), em.now("sp_bc")], sem="ld_bc")
            ecl = dma("sync", self.clo[:, :],
                      self.dram[f"s_bc_{d}"][:, (2 * ch + 1) * TC:
                                             (2 * ch + 2) * TC],
                      waits=[free.get("bchl"), em.now("sp_bc"), ech],
                      sem="ld_bc")
            eyp = dma("sync", self.pb1.rearrange("p (kc t) -> p kc t", kc=KD),
                      self.dram[f"s_yp_{d}"].rearrange("p (kc t) -> p kc t",
                                                       kc=KD)
                      [:, :, ch * TC:(ch + 1) * TC],
                      waits=[free.get("pb1"), em.now("sp_yp0"),
                             em.now("sp_yp1")], sem="ld_yp")
            for s_ in range(DS):
                seg = self.bca[:, (DS + s_) * TC:(DS + s_ + 1) * TC]
                half = (s_ % 2) * TC

                def mmbc(t, s_=s_, half=half):
                    t.matmul(psum[5][:, half:half + TC],
                             self.selw[0:16, s_ * 128:(s_ + 1) * 128],
                             self.chi[:, :], start=True, stop=False)
                    return t.matmul(psum[5][:, half:half + TC],
                                    self.selw[0:16, s_ * 128:(s_ + 1) * 128],
                                    self.clo[:, :], start=False, stop=True)
                E("tensor", mmbc,
                  waits=[ech, ecl, bank_r[5],
                         free.get("bca") if s_ == 0 else None], inc="t")
                bank_w[5] = em.now("t")
                E("scalar", lambda s, seg=seg, half=half:
                  s.activation(seg, psum[5][:, half:half + TC], AF.Copy),
                  waits=[bank_w[5]], inc="s")
                bank_r[5] = em.now("s")
            bca_ev = em.now("s")

            for ct in range(NCT):
                dl = self.dlt[ct % 2]

                def mmdtp(t, ct=ct, d=d):
                    return t.matmul(psum[4][:, :TC],
                                    self.dtpw[d][:, ct * 128:(ct + 1) * 128],
                                    self.dtbf[:, :], start=True, stop=True)
                E("tensor", mmdtp, waits=[edt, bank_r[4],
                                          free.get(f"dlt{ct % 2}")], inc="t")
                bank_w[4] = em.now("t")
                self.softplus(4, dl[:, :], self.sv(f"dtp_b_{d}")[:, ct:ct + 1])
                E("vector", lambda v, dl=dl, ct=ct: v.tensor_tensor_scan(
                    self.dut[:, :], dl[:, :], self.zerosT[:, :],
                    self.pcarry[:, ct:ct + 1], ALU.add, ALU.add),
                  waits=[free.get("dut")], inc="v")
                p_ev = em.now("v")
                E("vector", lambda v, ct=ct: v.tensor_copy(
                    self.pcarry[:, ct:ct + 1], self.dut[:, TC - 1:TC]),
                  inc="v")
                free[f"dlt{ct % 2}"] = em.now("v")

                acc_i = 0
                for sg in range(NSG):
                    ea = self.ett[sg % 2]

                    def se(s, ct=ct, sg=sg, d=d, ea=ea):
                        r = None
                        for sl_ in range(SG):
                            s_ = sg * SG + sl_
                            r = s.activation(
                                ea[:, sl_ * TC:(sl_ + 1) * TC],
                                self.dut[:, :], AF.Exp,
                                scale=self.Asb[d][:, ct * DS + s_:
                                                  ct * DS + s_ + 1])
                        return r
                    E("scalar", se, waits=[p_ev, free.get(f"corr_e{sg % 2}")],
                      inc="s")
                    e_ev = em.now("s")

                    def vmacc(v, sg=sg, ct=ct, d=d, ea=ea, acc_i=acc_i):
                        v.tensor_mul(self.ar_m[:, :], ea[:, :],
                                     self.bca[:, (DS + sg * SG) * TC:
                                              (DS + (sg + 1) * SG) * TC])
                        last = None
                        for sl_ in range(SG):
                            s_ = sg * SG + sl_
                            hcol = self.hin[d][:, ct * DS + s_:
                                               ct * DS + s_ + 1]
                            j = acc_i + sl_
                            if j == 0:
                                last = v.tensor_scalar(
                                    self.yacc[0][:, :],
                                    self.ar_m[:, sl_ * TC:(sl_ + 1) * TC],
                                    hcol, None, ALU.mult)
                            else:
                                last = v.scalar_tensor_tensor(
                                    self.yacc[j % 2][:, :],
                                    self.ar_m[:, sl_ * TC:(sl_ + 1) * TC],
                                    hcol, self.yacc[(j + 1) % 2][:, :],
                                    ALU.mult, ALU.add)
                        return last
                    E("vector", vmacc, waits=[e_ev, bca_ev, ev[f"hin_{d}"]],
                      inc="v")
                    free[f"corr_e{sg % 2}"] = em.now("v")
                    acc_i += SG
                free["dut"] = em.now("v")

                esz = dma("sync", self.szrel[ct % 2][:, :],
                          self.dram[f"s_sz_{d}"][:, ct * TOK + ch * TC:
                                                 ct * TOK + (ch + 1) * TC],
                          waits=[free.get(f"szl{ct % 2}"),
                                 em.now(f"sp_sz{ct % 2}")],
                          sem=f"ld_sz{ct % 2}")
                yfin = self.yacc[(DS - 1) % 2]
                E("vector", lambda v, ct=ct, yfin=yfin:
                  v.tensor_mul(self.g1ch[:, ct * TC:(ct + 1) * TC],
                               yfin[:, :], self.szrel[ct % 2][:, :]),
                  waits=[esz, free.get("g1_cons")], inc="v")
                free[f"szl{ct % 2}"] = em.now("v")
                ev[f"g1_{ct}"] = em.now("v")
            free["bca"] = em.now("v")

            for mt in range(KD):
                wbuf, wb16 = self.wst32[mt % 2], self.wst16[mt % 2]
                e = dma("sync",
                        wbuf[:, :NCT * 128].rearrange("p (kc m) -> p kc m",
                                                      kc=NCT),
                        outw[:, :, mt * 128:(mt + 1) * 128],
                        waits=[free.get(f"wst{mt % 2}")] +
                              ([free.get("bca")] if mt % 2 == 0 else []),
                        sem=f"ld_w{mt % 2}")
                E("vector", lambda v, wb16=wb16, wbuf=wbuf:
                  v.tensor_copy(wb16[:, :NCT * 128], wbuf[:, :NCT * 128]),
                  waits=[e, free.get(f"wst16_{mt % 2}")], inc="v")
                cast_ev = em.now("v")
                free[f"wst{mt % 2}"] = cast_ev
                bk = mt % 2

                def mmop(t, wb16=wb16, bk=bk):
                    r = None
                    for kc in range(NCT):
                        r = t.matmul(psum[bk][:, :TC],
                                     wb16[:, kc * 128:(kc + 1) * 128],
                                     self.g1ch[:, kc * TC:(kc + 1) * TC],
                                     start=(kc == 0), stop=(kc == NCT - 1))
                    return r
                E("tensor", mmop,
                  waits=[cast_ev, bank_r[bk]] +
                        [ev[f"g1_{c}"] for c in range(NCT)], inc="t")
                bank_w[bk] = em.now("t")
                free["g1_cons"] = bank_w[bk]
                free[f"wst16_{mt % 2}"] = bank_w[bk]
                yp = self.ypst[mt % 2]
                E("vector", lambda v, yp=yp, bk=bk, mt=mt:
                  v.tensor_add(yp[:, :], psum[bk][:, :TC],
                               self.pb1[:, mt * TC:(mt + 1) * TC]),
                  waits=[bank_w[bk], eyp, free.get(f"ypst{mt % 2}")], inc="v")
                bank_r[bk] = em.now("v")
                e = dma("gpsimd",
                        self.dram[f"s_yp_{d}"][:, mt * TOK + ch * TC:
                                               mt * TOK + (ch + 1) * TC],
                        yp[:, :], waits=[em.now("v")],
                        sem=f"sp_yp{mt % 2}")
                free[f"ypst{mt % 2}"] = e
            free["pb1"] = em.now("v")
            free["bchl"] = em.now("v")
            free["dtbf"] = em.now("v")

    def emit_final(self):
        E, dma, em, ev, free = self.E, self.dma, self.em, self.ev, self.free
        psum, bank_w, bank_r = self.psum, self.bank_w, self.bank_r
        x3 = self.dram["x_f"].rearrange("p (kc t) -> p kc t", kc=KD)
        projw = self.dram["proj_wT"].rearrange("p (kc m) -> p kc m", kc=NCT)
        ypf = self.pb1
        ypb = self.g1ch[:, 0:KD * TC]
        for ch in range(NCH):
            ef = dma("sync", ypf.rearrange("p (kc t) -> p kc t", kc=KD),
                     self.dram["s_yp_f"].rearrange("p (kc t) -> p kc t", kc=KD)
                     [:, :, ch * TC:(ch + 1) * TC],
                     waits=[free.get("pb1"), em.now("sp_yp0"),
                            em.now("sp_yp1")], sem="ld_yp")
            eb = dma("sync", ypb.rearrange("p (kc t) -> p kc t", kc=KD),
                     self.dram["s_yp_b"].rearrange("p (kc t) -> p kc t", kc=KD)
                     [:, :, TOK - (ch + 1) * TC:TOK - ch * TC],
                     waits=[free.get("g1_cons"), em.now("sp_yp0"),
                            em.now("sp_yp1")], sem="ld_ypb")
            exld = dma("sync", self.xch.rearrange("p (kc t) -> p kc t", kc=KD),
                       x3[:, :, 3 + ch * TC:3 + (ch + 1) * TC],
                       waits=[free.get("xch")], sem="ld_x")
            free["ln_dst"] = free.get("px2_cons")
            ln_ev = self.emit_ln(self.xch[:, :], TC, self.px2[:, :], None,
                                 self.lnr, self.lnhl, do_sum=False, sA=2,
                                 sB=3, xb=self.xbf[:, :], sq=self.ffnsq,
                                 src_ev=exld)
            free["xch"] = ln_ev
            eff = dma("sync", self.px1.rearrange("p (kc t) -> p kc t", kc=KD),
                      self.dram["s_ffn"].rearrange("p (kc t) -> p kc t", kc=KD)
                      [:, :, ch * TC:(ch + 1) * TC],
                      waits=[free.get("px1"), em.now("sp_ffn")], sem="ld_ffn")
            for mt in range(KD):
                wbuf, wb16 = self.wst32[mt % 2], self.wst16[mt % 2]
                e = dma("sync",
                        wbuf[:, :NCT * 128].rearrange("p (kc m) -> p kc m",
                                                      kc=NCT),
                        projw[:, :, mt * 128:(mt + 1) * 128],
                        waits=[free.get(f"wst{mt % 2}")], sem=f"ld_w{mt % 2}")
                E("vector", lambda v, wb16=wb16, wbuf=wbuf:
                  v.tensor_copy(wb16[:, :NCT * 128], wbuf[:, :NCT * 128]),
                  waits=[e, free.get(f"wst16_{mt % 2}")], inc="v")
                cast_ev = em.now("v")
                free[f"wst{mt % 2}"] = cast_ev
                bk = mt % 2

                def mmpr(t, wb16=wb16, bk=bk):
                    r = None
                    for kc in range(NCT):
                        if kc < KD:
                            rhs = ypf[:, kc * TC:(kc + 1) * TC]
                        else:
                            rhs = ypb[:, (kc - KD) * TC:
                                      (kc - KD + 1) * TC][:, ::-1]
                        r = t.matmul(psum[bk][:, :TC],
                                     wb16[:, kc * 128:(kc + 1) * 128], rhs,
                                     start=(kc == 0), stop=(kc == NCT - 1))
                    return r
                E("tensor", mmpr, waits=[cast_ev, ef, eb, bank_r[bk]], inc="t")
                bank_w[bk] = em.now("t")
                free[f"wst16_{mt % 2}"] = bank_w[bk]
                free["pb1"] = bank_w[bk]
                free["g1_cons"] = bank_w[bk]
                E("scalar", lambda s, bk=bk, mt=mt:
                  s.activation(self.mtt[0][:, :], psum[bk][:, :TC], AF.Identity,
                               bias=self.sv("proj_b")[:, mt:mt + 1]),
                  waits=[bank_w[bk], free.get("mtt0")], inc="s")
                bank_r[bk] = em.now("s")
                s_ev = em.now("s")
                E("scalar", lambda s, mt=mt:
                  s.activation(self.mtt[1][:, :],
                               self.px1[:, mt * TC:(mt + 1) * TC], AF.Silu),
                  waits=[eff, free.get("mtt1")], inc="s")
                g_ev = em.now("s")

                def vfin(v, mt=mt):
                    v.tensor_add(self.mtt[0][:, :], self.mtt[0][:, :],
                                 self.px2[:, mt * TC:(mt + 1) * TC])
                    v.tensor_mul(self.mtt[0][:, :], self.mtt[0][:, :],
                                 self.mtt[1][:, :])
                    return v.tensor_add(self.px2[:, mt * TC:(mt + 1) * TC],
                                        self.xch[:, mt * TC:(mt + 1) * TC],
                                        self.mtt[0][:, :])
                E("vector", vfin, waits=[s_ev, g_ev, ln_ev], inc="v")
                free["mtt0"] = em.now("v")
                free["mtt1"] = em.now("v")
            fin_ev = em.now("v")
            e = dma("gpsimd", self.dram["out"].rearrange(
                "p (kc t) -> p kc t", kc=KD)[:, :, ch * TC:(ch + 1) * TC],
                self.px2.rearrange("p (kc t) -> p kc t", kc=KD),
                waits=[fin_ev], sem="sp_out")
            free["px2_cons"] = e
            free["px1"] = g_ev
            free["xch"] = fin_ev

    def _replay(self):
        nc, em = self.nc, self.em

        class CountProxy:
            def __init__(self):
                self.n = 0

            def wait_ge(self, *a, **k):
                return None

            def nop(self, *a, **k):
                return _FakeInst()

            def __getattr__(self, name):
                def f(*a, **k):
                    self.n += 1
                    return _FakeInst()
                return f

        class _FakeInst:
            def then_inc(self, *a, **k):
                return self

        group_end = {}
        for eng in ("vector", "scalar"):
            cp = CountProxy()
            g = 0
            for fn, waits, inc, inc_n in em.streams[eng]:
                fn(cp)
                if inc is not None:
                    g += inc_n
                    group_end[(eng, g)] = cp.n

        ENG_OF_SEM = {"v": "vector", "s": "scalar"}

        class PipeProxy:

            def __init__(self, proxy, sem):
                self._p = proxy
                self._sem = sem
                self._n = 0
                self._ach = 0
                self._last_w = {}
                self._last_r = {}

            def wait_ge(self, *a, **k):
                return self._p.wait_ge(*a, **k)

            def nop(self, *a, **k):
                return self._p.nop(*a, **k)

            def _key(self, ap):
                try:
                    t = ap.tensor
                    rng = getattr(t, "manual_sbuf_range", None)
                    return (t.name, rng)
                except Exception:
                    return None

            def __getattr__(self, name):
                f = getattr(self._p, name)
                if not callable(f):
                    return f

                def overlap(r1, r2):
                    if r1 is None or r2 is None:
                        return True
                    return r1[0] < r2[1] and r2[0] < r1[1]

                def lookup(table, nm, rng):
                    need = 0
                    hit = table.get(nm)
                    if hit is not None:
                        need = hit[1]
                    if rng is not None:
                        for onm, (orng, oidx) in table.items():
                            if onm != nm and overlap(rng, orng):
                                need = max(need, oidx)
                    return need

                def wrapped(*a, **k):
                    aps = [x for x in a if isinstance(x, bass.AP)]
                    aps += [x for x in k.values() if isinstance(x, bass.AP)]
                    need = 0
                    outs = []
                    ins = []
                    for i, x in enumerate(aps):
                        key = self._key(x)
                        if key is None:
                            continue
                        nm, rng = key
                        if i == 0:
                            outs.append(key)
                            need = max(need, lookup(self._last_w, nm, rng),
                                       lookup(self._last_r, nm, rng))
                        else:
                            ins.append(key)
                            need = max(need, lookup(self._last_w, nm, rng))
                    if need > self._ach:
                        self._p.wait_ge(self._sem, need)
                        self._ach = need
                    inst = f(*a, **k)
                    if inst is not None:
                        inst.then_inc(self._sem, 1)
                        self._n += 1
                        for nm, rng in outs:
                            self._last_w[nm] = (rng, self._n)
                        for nm, rng in ins:
                            self._last_r[nm] = (rng, self._n)
                    return inst
                return wrapped

        with contextlib.ExitStack() as stack:
            sems = {name: stack.enter_context(nc.semaphore(f"sem_{name}"))
                    for name in em.count if name not in ("v", "s")}
            pipes = {e: stack.enter_context(nc.semaphore(f"pipe_{e}"))
                     for e in ("vector", "scalar")}

            def xlate(s, tgt):
                eng = ENG_OF_SEM.get(s)
                if eng is None:
                    return (sems[s], tgt)
                return (pipes[eng], group_end[(eng, tgt)] if tgt > 0 else 0)

            with nc.Block() as block:
                def mk(eng_name):
                    def runner(proxy):
                        if eng_name in pipes:
                            proxy = PipeProxy(proxy, pipes[eng_name])
                        ach = {}
                        for fn, waits, inc, inc_n in em.streams[eng_name]:
                            for s, tgt in waits:
                                h, t2 = xlate(s, tgt)
                                if t2 > ach.get(h, -1):
                                    proxy.wait_ge(h, t2)
                                    ach[h] = t2
                            last = fn(proxy)
                            if inc is not None and eng_name not in pipes:
                                assert last is not None, \
                                    f"{eng_name} inc without instruction"
                                last.then_inc(sems[inc], inc_n)
                    return runner
                block.sync(mk("sync"))
                block.tensor(mk("tensor"))
                block.vector(mk("vector"))
                block.scalar(mk("scalar"))
                block.gpsimd(mk("gpsimd"))


def kernel(**inputs):
    maps = host_prepare(inputs)
    if "nc" not in _NC_CACHE:
        _NC_CACHE["nc"] = build_nc()
    nc = _NC_CACHE["nc"]
    res = run_bass_kernel_spmd(nc, maps, core_ids=list(range(8)))
    out = np.empty((NB, L, D), np.float32)
    for core in range(8):
        beta, q = core // 4, core % 4
        o = res.results[core]["out"]
        o = o.reshape(128, KD, TOK).transpose(1, 0, 2).reshape(D, TOK)
        out[beta, q * TOK:(q + 1) * TOK] = o.T
    return out


# version 07 (final)
# speedup vs baseline: 1.5048x; 1.0234x over previous
import contextlib

import numpy as np

import concourse.bass as bass
import concourse.mybir as mybir
from concourse.bass_utils import run_bass_kernel_spmd

F32 = mybir.dt.float32
BF16 = mybir.dt.bfloat16
AF = mybir.ActivationFunctionType
ALU = mybir.AluOpType
AX = mybir.AxisListType

D = 1024
DI = 2048
DS = 16
DTR = 64
DC = 4
NB = 2
L = 8192
TOK = 2048
TC = 256
NCH = TOK // TC
SG = 4
NSG = DS // SG
NCT = DI // 128
KD = D // 128
EPS = 1e-5

ENGINES = ("sync", "tensor", "vector", "scalar", "gpsimd")
_DBG = []
_TRUNC = 0
_NC_CACHE = {}
SMALL_COLS = []


def _dedup(waits):
    best = {}
    for s, tgt in waits:
        best[s] = max(best.get(s, 0), tgt)
    return list(best.items())


class Emitter:
    def __init__(self):
        self.streams = {e: [] for e in ENGINES}
        self.count = {}
        self.dma_sems = set()

    def sem(self, name):
        self.count.setdefault(name, 0)
        return name

    def emit(self, eng, fn, waits=(), inc=None, inc_n=1):
        fixed = []
        for s, tgt in waits:
            assert tgt <= self.count[s], (
                f"forward wait: {eng} waits {s}>={tgt} emitted={self.count[s]}")
            if s in self.dma_sems:
                tgt = self.count[s]
            fixed.append((s, tgt))
        waits = _dedup(fixed)
        self.streams[eng].append((fn, tuple(waits), inc, inc_n))
        if inc is not None:
            self.count[inc] += inc_n
            return (inc, self.count[inc])
        return None

    def now(self, sem):
        return (sem, self.count[sem])


class SbufPool:
    def __init__(self, nc, size=200 * 1024):
        self.nc = nc
        backing = nc.alloc_sbuf_tensor("manualpool", [128, size], mybir.dt.uint8)
        base = nc.lookup_mloc(backing).addr
        self.off = base
        self.budget = base + size
        self.n = 0
        self.offsets = {}

    def _mk(self, name, shape, dtype, off):
        self.n += 1
        self.offsets[name] = off
        h = self.nc.alloc_sbuf_tensor_at(f"{name}_{self.n}", list(shape), dtype,
                                         offset=off)
        return h.ap()

    def alloc(self, name, shape, dtype):
        per_part = int(np.prod(shape[1:])) * mybir.dt.size(dtype)
        per_part = (per_part + 31) // 32 * 32
        off = self.off
        self.off += per_part
        assert self.off <= self.budget, (
            f"SBUF overflow at {name}: {self.off} > {self.budget}")
        return self._mk(name, shape, dtype, off)

    def alloc_at(self, name, shape, dtype, off):
        per_part = int(np.prod(shape[1:])) * mybir.dt.size(dtype)
        assert off + per_part <= self.budget, f"overlay overflow {name}"
        return self._mk(name, shape, dtype, off)


def _f32(x):
    return np.ascontiguousarray(x, dtype=np.float32)


def _lhsT_lay(w_T):
    K, M = w_T.shape
    if K >= 128:
        kc = K // 128
        return _f32(w_T.reshape(kc, 128, M).transpose(1, 0, 2).reshape(128, kc * M))
    return _f32(w_T)


def host_prepare(inputs):
    x = np.asarray(inputs["L0_attn_x"])

    def col_feat(v):
        return _f32(np.asarray(v).reshape(KD, 128).T)

    def col_inner(v):
        return _f32(np.asarray(v).reshape(NCT, 128).T)

    shared = {}
    for d in ("f", "b"):
        shared[f"in_wT_{d}"] = _lhsT_lay(np.asarray(inputs[f"L0_attn_in_w_{d}"]).T)
        shared[f"out_wT_{d}"] = _lhsT_lay(np.asarray(inputs[f"L0_attn_out_w_{d}"]).T)
        shared[f"xp_wT_{d}"] = _lhsT_lay(np.asarray(inputs[f"L0_attn_xp_w_{d}"]).T)
        shared[f"dtp_wT_{d}"] = _lhsT_lay(np.asarray(inputs[f"L0_attn_dtp_w_{d}"]).T)
        cw = np.asarray(inputs[f"L0_attn_conv_w_{d}"]).reshape(DI, DC)
        shared[f"conv_w_{d}"] = _f32(
            cw.reshape(NCT, 128, DC).transpose(1, 0, 2).reshape(128, NCT * DC))
        al = np.asarray(inputs[f"L0_attn_A_log_{d}"])
        shared[f"A_log_{d}"] = _f32(
            al.reshape(NCT, 128, DS).transpose(1, 0, 2).reshape(128, NCT * DS))
    shared["proj_wT"] = _lhsT_lay(np.asarray(inputs["L0_attn_proj_w"]).T)
    shared["cn_w1"] = _lhsT_lay(np.asarray(inputs["L0_ffn_cn_w1"]))
    shared["cn_w2"] = _lhsT_lay(np.asarray(inputs["L0_ffn_cn_w2"]))
    shared["cn_w3"] = _lhsT_lay(np.asarray(inputs["L0_ffn_cn_w3"]))
    w4 = np.asarray(inputs["L0_ffn_cn_w4"])
    shared["cn_w4"] = _lhsT_lay(np.concatenate([w4, w4], axis=1))
    shared["gc_w1"] = _lhsT_lay(np.asarray(inputs["L0_ffn_gc_w1"]))
    shared["gc_w2"] = _lhsT_lay(np.asarray(inputs["L0_ffn_gc_w2"]))
    w3g = np.asarray(inputs["L0_ffn_gc_w3"])
    shared["gc_w3"] = _lhsT_lay(np.concatenate([w3g, w3g], axis=1))
    sel = np.zeros((128, DS * 128), np.float32)
    for s in range(DS):
        sel[s, s * 128:(s + 1) * 128] = 1.0
    shared["sel"] = sel

    cols = {}
    cols["ln_g"] = col_feat(inputs["L0_attn_ln_gamma"])
    cols["ln_b"] = col_feat(inputs["L0_attn_ln_beta"])
    cols["proj_b"] = col_feat(inputs["L0_attn_proj_b"])
    for d in ("f", "b"):
        cols[f"conv_b_{d}"] = col_inner(inputs[f"L0_attn_conv_b_{d}"])
        cols[f"dtp_b_{d}"] = col_inner(inputs[f"L0_attn_dtp_b_{d}"])
        cols[f"Dk_{d}"] = col_inner(inputs[f"L0_attn_D_{d}"])
    cols["cn_b1"] = col_feat(inputs["L0_ffn_cn_b1"])
    cols["cn_b2"] = _f32(np.asarray(inputs["L0_ffn_cn_b2"]).reshape(4, 128).T)
    cols["cn_b3"] = _f32(np.asarray(inputs["L0_ffn_cn_b3"]).reshape(2, 128).T)
    cols["cn_b4"] = np.full(
        (2, 1), float(np.asarray(inputs["L0_ffn_cn_b4"]).reshape(-1)[0]),
        np.float32)
    cols["gc_b1"] = _f32(np.asarray(inputs["L0_ffn_gc_b1"]).reshape(4, 128).T)
    cols["gc_b2"] = _f32(np.asarray(inputs["L0_ffn_gc_b2"]).reshape(2, 128).T)
    cols["gc_b3"] = np.full(
        (2, 1), float(np.asarray(inputs["L0_ffn_gc_b3"]).reshape(-1)[0]),
        np.float32)
    cols["eps"] = np.full((128, 1), EPS, np.float32)
    cols["one"] = np.full((128, 1), 1.0, np.float32)
    cols["ar"] = np.full(
        (2, 1), float(np.asarray(inputs["L0_ffn_adaptive_range"]).reshape(-1)[0]),
        np.float32)

    maps = []
    for core in range(8):
        beta, q = core // 4, core % 4
        sl = x[beta, q * TOK:(q + 1) * TOK]
        prev3 = x[beta, q * TOK - 3:q * TOK] if q > 0 else x[beta, 0:3]
        nxt3 = (x[beta, (q + 1) * TOK:(q + 1) * TOK + 3] if q < 3
                else x[beta, L - 3:L])

        def featmaj(a):
            T = a.shape[0]
            return _f32(a.T.reshape(KD, 128, T).transpose(1, 0, 2)
                        .reshape(128, KD * T))

        percols = dict(cols)
        percols["bmask_f"] = np.full((128, 1), 0.0 if q == 0 else 1.0, np.float32)
        percols["bmask_b"] = np.full((128, 1), 0.0 if q == 3 else 1.0, np.float32)
        oh_f = np.zeros((128, 4), np.float32)
        if q >= 1:
            oh_f[:, q] = 1.0
        percols["oh_f"] = oh_f
        oh_b = np.zeros((128, 4), np.float32)
        if q <= 2:
            oh_b[:, 3 - q] = 1.0
        percols["oh_b"] = oh_b

        global SMALL_COLS
        if not SMALL_COLS:
            SMALL_COLS = [(k, percols[k].shape[1]) for k in percols]
        small = np.zeros((128, sum(w for _, w in SMALL_COLS)), np.float32)
        o = 0
        for k, w in SMALL_COLS:
            a = percols[k]
            small[: a.shape[0], o:o + w] = a
            o += w

        m = dict(shared)
        m["x_f"] = featmaj(np.concatenate([prev3, sl], 0))
        m["x_b"] = featmaj(np.concatenate([nxt3[::-1], sl[::-1]], 0))
        m["small"] = small
        maps.append(m)
    return maps


def small_index():
    idx = {}
    o = 0
    for k, w in SMALL_COLS:
        idx[k] = (o, w)
        o += w
    return idx, o



def build_nc():
    SIX, NSM = small_index()
    nc = bass.Bass("TRN2", target_bir_lowering=False)
    em = Emitter()
    for s in ("ld", "sp", "t", "v", "s", "cc"):
        em.sem(s)

    dram = {}

    def din(name, shape, dtype=F32):
        dram[name] = nc.dram_tensor(name, list(shape), dtype,
                                    kind="ExternalInput").ap()

    def dint(name, shape, dtype=F32):
        dram[name] = nc.dram_tensor(name, list(shape), dtype, kind="Internal").ap()

    din("x_f", [128, KD * (TOK + 3)])
    din("x_b", [128, KD * (TOK + 3)])
    din("small", [128, NSM])
    din("sel", [128, DS * 128])
    for d in ("f", "b"):
        din(f"in_wT_{d}", [128, KD * 2 * DI])
        din(f"out_wT_{d}", [128, NCT * D])
        din(f"xp_wT_{d}", [128, NCT * 96])
        din(f"dtp_wT_{d}", [DTR, DI])
        din(f"conv_w_{d}", [128, NCT * DC])
        din(f"A_log_{d}", [128, NCT * DS])
        dint(f"s_xc_{d}", [128, NCT * TOK])
        dint(f"s_sz_{d}", [128, NCT * TOK])
        dint(f"s_dt_{d}", [DTR, TOK], BF16)
        dint(f"s_bc_{d}", [16, 2 * TOK], BF16)
        dint(f"s_yp_{d}", [128, KD * TOK], BF16)
        dint(f"cc_in_{d}", [128, 2 * NCT * DS])
        dint(f"cc_out_{d}", [512, 2 * NCT * DS])
    din("proj_wT", [128, NCT * D])
    din("cn_w1", [128, KD * D])
    din("cn_w2", [128, KD * 512])
    din("cn_w3", [128, 4 * 256])
    din("cn_w4", [128, 2 * 2])
    din("gc_w1", [128, KD * 512])
    din("gc_w2", [128, 4 * 256])
    din("gc_w3", [128, 2 * 2])
    dint("s_ffn", [128, KD * TOK])
    dint("cc_in_m", [128, KD])
    dint("cc_out_m", [128, KD])
    dram["out"] = nc.dram_tensor("out", [128, KD * TOK], F32,
                                 kind="ExternalOutput").ap()
    dbg = {}
    for name, shape, dtype in _DBG:
        dbg[name] = nc.dram_tensor("dbg_" + name, list(shape), dtype,
                                   kind="ExternalOutput").ap()
        em.sem("sp")

    with contextlib.ExitStack() as stack:
        psum = [stack.enter_context(nc.psum_tensor(f"ps{i}", [128, 512],
                                                   F32)).ap()
                for i in range(8)]
        _Prog(nc, em, dram, dbg, psum, SIX, NSM).build()
    return nc


class _Prog:
    def __init__(self, nc, em, dram, dbg, psum, SIX, NSM):
        self.nc, self.em, self.dram, self.dbg = nc, em, dram, dbg
        self.psum = psum
        self.SIX, self.NSM = SIX, NSM
        self.ev = {}
        self.bank_w = [None] * 8
        self.bank_r = [None] * 8
        self.free = {}
        self.sb = SbufPool(nc)

    def E(self, eng, fn, waits=(), inc=None, n=1):
        return self.em.emit(eng, fn, waits=[w for w in waits if w is not None],
                            inc=inc, inc_n=n)

    def dma(self, queue, out, in_, waits=(), sem=None):
        s = sem or ("ld" if queue == "sync" else "sp")
        self.em.sem(s)
        self.em.dma_sems.add(s)

        def fn(eng, out=out, in_=in_):
            return eng.dma_start(out=out, in_=in_)
        return self.em.emit(queue, fn,
                            waits=[w for w in waits if w is not None],
                            inc=s, inc_n=16)

    def sv(self, name):
        o, n = self.SIX[name]
        return self.small[:, o:o + n]

    def tap(self, name, ap, wait_ev=None, guard=None):
        if name in self.dbg:
            nc_ = int(np.prod(ap.shape[1:]))
            np_ = ap.shape[0]
            stage = self.px2[0:np_, 0:nc_] if ap.dtype == F32 else \
                self.pb1[0:np_, 0:nc_]
            self.E("vector", lambda v: v.tensor_copy(stage, ap),
                   waits=[wait_ev, self.free.get("tap")], inc="v")
            e = self.dma("gpsimd", self.dbg[name][:, :], stage,
                         waits=[self.em.now("v")], sem="sp_dbg")
            self.free["tap"] = e

    def build(self):
        self.alloc_all()
        self.emit_consts()
        if _TRUNC:
            self.emit_main("f")
            em = self.em
            allsems = [em.now(s) for s in sorted(em.count)
                       if s.startswith(("ld_", "sp_"))]
            self.E("gpsimd", lambda g: g.nop(), waits=allsems)
            self._replay()
            return
        self.emit_main("f")
        self.emit_xnsum_cc()
        self.emit_main("b")
        self.emit_ffn()
        self.emit_combine("f")
        self.emit_corr("f")
        self.emit_combine("b")
        self.emit_corr("b")
        self.emit_final()
        em = self.em
        allsems = [em.now(s) for s in sorted(em.count)
                   if s.startswith(("ld_", "sp_")) or s == "cc"]
        self.E("gpsimd", lambda g: g.nop(), waits=allsems)
        self._replay()

    def alloc_all(self):
        sb = self.sb
        self.small = sb.alloc("small", [128, self.NSM], F32)
        self.Asb = {d: sb.alloc(f"A_{d}", [128, NCT * DS], F32) for d in "fb"}
        self.convw = {d: sb.alloc(f"cw_{d}", [128, NCT * DC], F32) for d in "fb"}
        self.xpw = {d: sb.alloc(f"xpw_{d}", [128, NCT * 96], BF16) for d in "fb"}
        self.dtpw = {d: sb.alloc(f"dtpw_{d}", [DTR, DI], BF16) for d in "fb"}
        self.selw = sb.alloc("selw", [128, DS * 128], BF16)
        self.ones2 = sb.alloc("ones2", [2, 128], BF16)
        self.ones128 = sb.alloc("ones128", [128, 2], BF16)
        self.zerosT = sb.alloc("zerosT", [128, TC], F32)
        self.hstate = {d: sb.alloc(f"hst_{d}", [128, NCT * DS], F32) for d in "fb"}
        self.sumdl = {d: sb.alloc(f"sdl_{d}", [128, NCT], F32) for d in "fb"}
        self.hin = {d: sb.alloc(f"hin_{d}", [128, NCT * DS], F32) for d in "fb"}
        self.bound = {d: sb.alloc(f"bnd_{d}", [128, NCT * 4], F32) for d in "fb"}
        self.xnsum = sb.alloc("xnsum", [128, KD], F32)
        self.gfrow = sb.alloc("gfrow", [2, 8], F32)
        self.pcarry = sb.alloc("pcarry", [128, NCT], F32)
        self.red1 = sb.alloc("red1", [128, 1], F32)
        self.lnhl = sb.alloc("lnhl", [2, TC], BF16)
        self.lnlo = sb.alloc("lnlo", [2, TC], BF16)
        self.lnr = [sb.alloc(f"lnr_{i}", [2, TC], F32) for i in range(5)]
        self.tkt = sb.alloc("tkt", [128, NCT * DS], F32)
        self.cctmp = sb.alloc("cctmp", [128, 2 * NCT * DS], F32)
        self.hpre = [sb.alloc(f"hpre_{i}", [128, NCT * DS], F32) for i in (0, 1)]

        r0 = sb.off
        self.xch = sb.alloc("xch", [128, KD * TC], F32)
        self.xbf = sb.alloc("xbf", [128, KD * TC], BF16)
        self.xn2bf = sb.alloc("xn2bf", [128, KD * TC], BF16)
        self.xcbf = sb.alloc("xcbf", [128, NCT * TC], BF16)
        self.wstA = sb.alloc("wstA", [128, KD * 256], F32)
        assert sb.off - r0 == 32 * 1024, sb.off - r0
        self.bca = sb.alloc_at("bca", [128, 2 * DS * TC], F32, r0)
        self.sqbf = self.xcbf[:, 0:KD * TC]

        self.wstB = sb.alloc("wstB", [128, KD * 256], F32)
        self.wst32 = [self.wstA, self.wstB]
        self.wst16 = [sb.alloc(f"wst16_{i}", [128, KD * 256], BF16)
                      for i in (0, 1)]

        self.xe3 = sb.alloc("xe3", [128, KD * 3], F32)
        self.xbfe = sb.alloc("xbfe", [128, KD * 3], BF16)
        self.sqbfe = sb.alloc("sqbfe", [128, KD * 3], BF16)
        self.xn2e = sb.alloc("xn2e", [128, KD * 3], BF16)
        self.lnre = [sb.alloc(f"lnre_{i}", [2, 3], F32) for i in range(5)]
        self.lnhle = sb.alloc("lnhle", [2, 3], BF16)
        self.lnhle2 = sb.alloc("lnhle2", [2, 3], BF16)
        self.xcpre = [sb.alloc(f"xcpre_{i}", [128, TC + 3], F32) for i in (0, 1)]
        self.convt = [sb.alloc(f"convt_{i}", [128, TC], F32) for i in (0, 1)]
        self.xcring = [sb.alloc(f"xcr_{i}", [128, TC], F32) for i in (0, 1)]
        self.szring = [sb.alloc(f"szr_{i}", [128, TC], F32) for i in (0, 1)]
        self.xcrel = [sb.alloc(f"xcl_{i}", [128, TC], F32) for i in (0, 1)]
        self.szrel = [sb.alloc(f"szl_{i}", [128, TC], F32) for i in (0, 1)]
        self.dtbf = sb.alloc("dtbf", [DTR, TC], BF16)
        self.bhi = sb.alloc("bhi", [16, TC], BF16)
        self.blo = sb.alloc("blo", [16, TC], BF16)
        self.chi = sb.alloc("chi", [16, TC], BF16)
        self.clo = sb.alloc("clo", [16, TC], BF16)
        self.bclo = sb.alloc("bclo", [16, TC], F32)
        self.dlt = [sb.alloc(f"dlt_{i}", [128, TC], F32) for i in (0, 1)]
        self.sp1 = sb.alloc("sp1", [128, TC], F32)
        self.sp2 = sb.alloc("sp2", [128, TC], F32)
        self.dut = sb.alloc("dut", [128, TC], F32)
        a0 = sb.off
        self.ar_a2 = [sb.alloc(f"ar_a{i}", [128, SG * TC], F32) for i in (0, 1)]
        self.ffnsq = sb.alloc_at("ffnsq", [128, KD * TC], BF16, a0)
        self.ar_b = sb.alloc("ar_b", [128, SG * TC], F32)
        self.ar_h = sb.alloc("ar_h", [128, SG * TC], F32)
        self.ar_m = sb.alloc("ar_m", [128, SG * TC], F32)
        self.fold = sb.alloc("fold", [128, 2 * TC], F32)
        self.ett = [sb.alloc(f"ett_{i}", [128, SG * TC], F32) for i in (0, 1)]
        self.mtt = [sb.alloc(f"mtt_{i}", [128, TC], F32) for i in (0, 1)]
        self.yacc = [sb.alloc(f"yacc_{i}", [128, TC], F32) for i in (0, 1)]
        self.g1ch = sb.alloc("g1ch", [128, NCT * TC], BF16)
        self.ypst = [sb.alloc(f"ypst_{i}", [128, TC], BF16) for i in (0, 1)]
        self.px1 = sb.alloc("px1", [128, KD * TC], F32)
        self.px2 = sb.alloc("px2", [128, KD * TC], F32)
        self.pb1 = sb.alloc("pb1", [128, KD * TC], BF16)

    def emit_consts(self):
        E, dma, em, ev, free = self.E, self.dma, self.em, self.ev, self.free
        ev["small"] = dma("sync", self.small[:, :], self.dram["small"][:, :], sem="ld_aux")
        prev_aux = ev["small"]
        e = dma("sync", self.wstB[:, :DS * 128], self.dram["sel"][:, :], sem="ld_w1")
        E("vector", lambda v: v.tensor_copy(self.selw[:, :],
                                            self.wstB[:, :DS * 128]),
          waits=[e], inc="v")
        free["wst1"] = em.now("v")
        for d in "fb":
            prev_aux = dma("sync", self.convw[d][:, :],
                           self.dram[f"conv_w_{d}"][:, :], sem="ld_aux",
                           waits=[prev_aux])
            ea = dma("sync", self.Asb[d][:, :], self.dram[f"A_log_{d}"][:, :],
                     sem="ld_aux", waits=[prev_aux])
            prev_aux = ea
            e1 = dma("sync", self.wstB[:, :NCT * 96],
                     self.dram[f"xp_wT_{d}"][:, :], waits=[free.get("wst1")],
                     sem="ld_w1")
            E("vector", lambda v, d=d: v.tensor_copy(self.xpw[d][:, :],
                                                     self.wstB[:, :NCT * 96]),
              waits=[e1], inc="v")
            free["wst1"] = em.now("v")
            e2 = dma("sync", self.wstA[:DTR, :DI],
                     self.dram[f"dtp_wT_{d}"][:, :], waits=[free.get("wst0")],
                     sem="ld_w0")
            E("vector", lambda v, d=d: v.tensor_copy(self.dtpw[d][:, :],
                                                     self.wstA[:DTR, :DI]),
              waits=[e2], inc="v")
            free["wst0"] = em.now("v")
            E("scalar", lambda s, d=d: s.activation(self.Asb[d][:, :],
                                                    self.Asb[d][:, :], AF.Exp),
              waits=[ea], inc="s")
            E("vector", lambda v, d=d: v.tensor_scalar(
                self.Asb[d][:, :], self.Asb[d][:, :], -1.0, None, ALU.mult),
              waits=[em.now("s")], inc="v")
            ev[f"A_{d}"] = em.now("v")

        def vinit(v):
            v.memset(self.ones2[:, :], 1.0)
            v.memset(self.ones128[:, :], 1.0)
            v.memset(self.zerosT[:, :], 0.0)
            for d in "fb":
                v.memset(self.hstate[d][:, :], 0.0)
                v.memset(self.sumdl[d][:, :], 0.0)
                v.memset(self.hin[d][:, :], 0.0)
            v.memset(self.xnsum[:, :], 0.0)
            return v.memset(self.pcarry[:, :], 0.0)
        E("vector", vinit, inc="v")
        ev["init"] = em.now("v")
        E("vector", lambda v: v.memset(self.red1[:, :], 0.0),
          waits=[ev["small"]], inc="v")
        ev["svr"] = em.now("v")

    def emit_ln(self, xsrc, ncols, dst_xn, dst_xn2, rows, hl, do_sum,
               sA, sB, xb, sq, src_ev=None):
        E, em, ev, free = self.E, self.em, self.ev, self.free
        psum, bank_w, bank_r = self.psum, self.bank_w, self.bank_r
        E("vector", lambda v: v.tensor_copy(xb, xsrc),
          waits=[src_ev, free.get("stats_xb")], inc="v")
        cast_ev = em.now("v")
        def ssq(s):
            r = None
            for kc in range(KD):
                r = s.activation(sq[:, kc * ncols:(kc + 1) * ncols],
                                 xsrc[:, kc * ncols:(kc + 1) * ncols],
                                 AF.Square)
            return r
        E("scalar", ssq, waits=[src_ev, free.get("stats_sq")], inc="s")
        sq_ev = em.now("s")
        pa, pb = psum[sA], psum[sB]

        def mmstats(t):
            r = None
            for kc in range(KD):
                r = t.matmul(pa[0:2, :ncols], self.ones128[:, :],
                             xb[:, kc * ncols:(kc + 1) * ncols],
                             start=(kc == 0), stop=(kc == KD - 1))
            for kc in range(KD):
                r = t.matmul(pa[32:34, :ncols], self.ones128[:, :],
                             sq[:, kc * ncols:(kc + 1) * ncols],
                             start=(kc == 0), stop=(kc == KD - 1))
            return r
        if "xbfdbg" in self.dbg:
            E("vector", lambda v: v.tensor_copy(self.px1[:, 0:KD * ncols], xb),
              waits=[cast_ev, sq_ev, self.free.get("tap")], inc="v")
            self.tap("xbfdbg", self.px1[:, 0:KD * ncols], em.now("v"))
            E("vector", lambda v: v.tensor_copy(self.px1[:, 0:KD * ncols], sq),
              waits=[self.free.get("tap")], inc="v")
            self.tap("sqdbg", self.px1[:, 0:KD * ncols], em.now("v"))
            del self.dbg["xbfdbg"]
        E("tensor", mmstats, waits=[cast_ev, sq_ev, bank_r[sA], ev["init"]],
          inc="t")
        bank_w[sA] = em.now("t")
        free["stats_xb"] = bank_w[sA]
        free["stats_sq"] = bank_w[sA]
        mu = rows[0][:, :ncols]
        var = rows[1][:, :ncols]
        t3 = rows[2][:, :ncols]
        rs = rows[3][:, :ncols]
        rr = rows[4][:, :ncols]

        if "statsdbg" in self.dbg:
            def vst(v):
                v.tensor_copy(self.px1[0:2, 0:ncols], pa[0:2, :ncols])
                return v.tensor_copy(self.px1[0:2, TC:TC + ncols],
                                     pa[32:34, :ncols])
            E("vector", vst, waits=[bank_w[sA], self.free.get("tap")], inc="v")
            self.tap("statsdbg", self.px1[:, 0:2 * TC], em.now("v"))
            del self.dbg["statsdbg"]

        def vstats(v):
            v.tensor_scalar(mu, pa[0:2, :ncols], 1.0 / D, None, ALU.mult)
            v.tensor_scalar(var, pa[32:34, :ncols], 1.0 / D, None, ALU.mult)
            v.tensor_mul(t3, mu, mu)
            return v.tensor_sub(var, var, t3)
        E("vector", vstats, waits=[bank_w[sA]], inc="v")
        bank_r[sA] = em.now("v")
        E("scalar", lambda s: s.activation(t3, var, AF.Sqrt,
                                           bias=self.sv("eps")[0:2, :]),
          waits=[em.now("v"), ev["svr"]], inc="s")
        E("vector", lambda v: v.reciprocal(rs, t3), waits=[em.now("s")], inc="v")
        if dst_xn2 is not None:
            def vln2(v):
                v.tensor_mul(rr, rs, rs)
                return v.tensor_mul(rr, rr, var)
            E("vector", vln2, inc="v")
            E("scalar", lambda s: s.activation(t3, rr, AF.Sqrt,
                                               bias=self.sv("eps")[0:2, :]),
              waits=[em.now("v")], inc="s")
            E("vector", lambda v: v.reciprocal(rr, t3), waits=[em.now("s")],
              inc="v")
            E("vector", lambda v: v.tensor_mul(rr, rr, rs), inc="v")
        segs = [(mu, pb, 0), (rs, pb, ncols)]
        if dst_xn2 is not None:
            segs.append((rr, pa, 0))
        lo = self.lnlo if hl is self.lnhl else self.lnhle2
        prev_mmb = None
        for i, (src, dstp, off) in enumerate(segs):
            def vhl(v, src=src):
                v.tensor_copy(hl[:, :ncols], src)
                v.tensor_sub(t3, src, hl[:, :ncols])
                return v.tensor_copy(lo[:, :ncols], t3)
            E("vector", vhl,
              waits=[bank_r[sB] if (dstp is pb and i == 0) else None,
                     prev_mmb], inc="v")
            hl_ev = em.now("v")

            def mmb(t, dstp=dstp, off=off):
                t.matmul(dstp[:, off:off + ncols], self.ones2[0:1, :],
                         hl[0:1, :ncols], start=True, stop=False)
                return t.matmul(dstp[:, off:off + ncols], self.ones2[0:1, :],
                                lo[0:1, :ncols], start=False, stop=True)
            E("tensor", mmb,
              waits=[hl_ev, bank_r[sB] if dstp is pb else bank_r[sA]], inc="t")
            prev_mmb = em.now("t")
            if dstp is pb:
                bank_w[sB] = em.now("t")
            else:
                bank_w[sA] = em.now("t")

        def vapply(v):
            last = None
            for kc in range(KD):
                xs = xsrc[:, kc * ncols:(kc + 1) * ncols]
                t0 = self.convt[0][:, :ncols]
                t1 = self.convt[1][:, :ncols]
                v.tensor_sub(t0, xs, pb[:, 0:ncols])
                if dst_xn is not None:
                    last = v.tensor_mul(dst_xn[:, kc * ncols:(kc + 1) * ncols],
                                        t0, pb[:, ncols:2 * ncols])
                if dst_xn2 is not None:
                    v.tensor_mul(t1, t0, pa[:, 0:ncols])
                    last = v.tensor_scalar(
                        dst_xn2[:, kc * ncols:(kc + 1) * ncols], t1,
                        self.sv("ln_g")[:, kc:kc + 1],
                        self.sv("ln_b")[:, kc:kc + 1], ALU.mult, ALU.add)
                if do_sum:
                    if dst_xn is not None:
                        xn_src = dst_xn[:, kc * ncols:(kc + 1) * ncols]
                    else:
                        v.tensor_mul(t1, t0, pb[:, ncols:2 * ncols])
                        xn_src = t1
                    v.reduce_sum(self.red1[:, :], xn_src, AX.X)
                    last = v.tensor_add(self.xnsum[:, kc:kc + 1],
                                        self.xnsum[:, kc:kc + 1],
                                        self.red1[:, :])
            return last
        if "lnrows" in self.dbg:
            def vrows(v):
                v.tensor_copy(self.px2[0:2, 0:ncols], mu)
                v.tensor_copy(self.px2[0:2, TC:TC + ncols], rs)
                return v.tensor_copy(self.px2[0:2, 2 * TC:2 * TC + ncols],
                                     rr if dst_xn2 is not None else var)
            E("vector", vrows, waits=[self.free.get("tap")], inc="v")
            e = self.dma("gpsimd", self.dbg["lnrows"][:, :],
                         self.px2[0:2, 0:3 * TC], waits=[em.now("v")],
                         sem="sp_dbg")
            self.free["tap"] = e
            del self.dbg["lnrows"]
        E("vector", vapply,
          waits=[bank_w[sA], bank_w[sB], free.get("convt"),
                 free.get("ln_dst"), self.ev["svr"]], inc="v")
        bank_r[sA] = em.now("v")
        bank_r[sB] = em.now("v")
        free["convt"] = em.now("v")
        return em.now("v")

    def softplus(self, pbank, dst, biascol):
        E, em, free = self.E, self.em, self.free
        pp = self.psum[pbank]

        def s1(s):
            s.activation(self.sp1[:, :], pp[:, :TC], AF.Abs, bias=biascol)
            s.activation(self.sp1[:, :], self.sp1[:, :], AF.Exp, scale=-1.0)
            return s.activation(self.sp2[:, :], self.sp1[:, :], AF.Ln, bias=self.sv("one"))
        E("scalar", s1, waits=[self.bank_w[pbank], free.get("sp")], inc="s")
        s_ev = em.now("s")

        def v1(v):
            v.tensor_scalar(self.sp1[:, :], pp[:, :TC], biascol, None, ALU.add)
            v.tensor_scalar(self.sp1[:, :], self.sp1[:, :], 0.0, None, ALU.max)
            return v.tensor_add(dst, self.sp1[:, :], self.sp2[:, :])
        E("vector", v1, waits=[s_ev], inc="v")
        self.bank_r[pbank] = em.now("v")
        free["sp"] = em.now("v")
        return em.now("v")

    def emit_main(self, d):
        E, dma, em, ev, free = self.E, self.dma, self.em, self.ev, self.free
        psum, bank_w, bank_r = self.psum, self.bank_w, self.bank_r
        x3 = self.dram[f"x_{d}"].rearrange("p (kc t) -> p kc t", kc=KD)
        inw = self.dram[f"in_wT_{d}"].rearrange("p (kc m) -> p kc m", kc=KD)
        outw = self.dram[f"out_wT_{d}"].rearrange("p (kc m) -> p kc m", kc=NCT)
        ln_e_ev = None

        for ch in range(_TRUNC or NCH):
            exld = dma("sync", self.xch.rearrange("p (kc t) -> p kc t", kc=KD),
                       x3[:, :, 3 + ch * TC:3 + (ch + 1) * TC],
                       waits=[free.get("xch"), free.get("bca")], sem="ld_x")
            free["ln_dst"] = free.get("xn2bf_cons")
            ln_ev = self.emit_ln(self.xch[:, :], TC, None, self.xn2bf,
                                 self.lnr, self.lnhl, do_sum=(d == "f"),
                                 sA=2, sB=3, xb=self.xbf[:, :], sq=self.sqbf,
                                 src_ev=exld)
            free["xch"] = ln_ev
            if ch == 0:
                self.tap(f"xn2bf_{d}", self.xn2bf[:, :], ln_ev,
                         guard="bca")
                e3 = dma("sync", self.xe3.rearrange("p (kc t) -> p kc t", kc=KD),
                         x3[:, :, 0:3], sem="ld_xe")
                ln_e_ev = self.emit_ln(self.xe3[:, :], 3, None, self.xn2e,
                                       self.lnre, self.lnhle, do_sum=False,
                                       sA=6, sB=7, xb=self.xbfe[:, :],
                                       sq=self.sqbfe[:, :], src_ev=e3)

            for mb in range(16):
                wbuf, wb16 = self.wst32[mb % 2], self.wst16[mb % 2]
                e = dma("sync", wbuf.rearrange("p (kc m) -> p kc m", kc=KD),
                        inw[:, :, mb * 256:(mb + 1) * 256],
                        waits=[free.get(f"wst{mb % 2}")] +
                              ([free.get("bca")] if mb % 2 == 0 else []),
                        sem=f"ld_w{mb % 2}")
                E("vector", lambda v, wb16=wb16, wbuf=wbuf:
                  v.tensor_copy(wb16[:, :], wbuf[:, :]),
                  waits=[e, free.get(f"wst16_{mb % 2}")], inc="v")
                cast_ev = em.now("v")
                free[f"wst{mb % 2}"] = cast_ev
                for sub in range(2):
                    mt = mb * 2 + sub
                    bk = mt % 4
                    is_xc = mt < NCT
                    ct = mt if is_xc else mt - NCT

                    def mmg(t, wb16=wb16, sub=sub, bk=bk, is_xc=is_xc, ch=ch):
                        r = None
                        for kc in range(KD):
                            lh = wb16[:, kc * 256 + sub * 128:
                                      kc * 256 + (sub + 1) * 128]
                            r = t.matmul(psum[bk][:, :TC], lh,
                                         self.xn2bf[:, kc * TC:(kc + 1) * TC],
                                         start=(kc == 0), stop=(kc == KD - 1))
                        if ch == 0 and is_xc:
                            for kc in range(KD):
                                lh = wb16[:, kc * 256 + sub * 128:
                                          kc * 256 + (sub + 1) * 128]
                                r = t.matmul(psum[6][:, 0:3], lh,
                                             self.xn2e[:, kc * 3:(kc + 1) * 3],
                                             start=(kc == 0), stop=(kc == KD - 1))
                        return r
                    E("tensor", mmg,
                      waits=[cast_ev, ln_ev, bank_r[bk]] +
                            ([ln_e_ev, bank_r[6]] if (ch == 0 and is_xc) else []),
                      inc="t")
                    bank_w[bk] = em.now("t")
                    free["xn2bf_cons"] = bank_w[bk]
                    free[f"wst16_{mb % 2}"] = bank_w[bk]
                    if ch == 0 and is_xc:
                        bank_w[6] = em.now("t")

                    if is_xc:
                        xr = self.xcpre[ct % 2]
                        E("scalar", lambda s, xr=xr, bk=bk:
                          s.activation(xr[:, 3:3 + TC], psum[bk][:, :TC],
                                       AF.Copy),
                          waits=[bank_w[bk], free.get(f"xcpre{ct % 2}")],
                          inc="s")
                        bank_r[bk] = em.now("s")
                        ev_ev = em.now("s")
                        if ch == 0:
                            E("vector", lambda v, xr=xr, d=d:
                              v.tensor_scalar(xr[:, 0:3], psum[6][:, 0:3],
                                              self.sv(f"bmask_{d}")[:, 0:1],
                                              None, ALU.mult),
                              waits=[bank_w[6], ev_ev, ev["svr"]], inc="v")
                            bank_r[6] = em.now("v")
                        else:
                            E("vector", lambda v, xr=xr, ct=ct, d=d:
                              v.tensor_copy(xr[:, 0:3],
                                            self.bound[d][:, ct * 4:ct * 4 + 3]),
                              waits=[ev_ev], inc="v")

                        def vconv(v, xr=xr, ct=ct, d=d):
                            w0 = self.convw[d][:, ct * DC:ct * DC + 1]
                            cb = self.sv(f"conv_b_{d}")[:, ct:ct + 1]
                            v.tensor_scalar(self.convt[0][:, :], xr[:, 0:TC],
                                            w0, cb, ALU.mult, ALU.add)
                            for j in (1, 2, 3):
                                wj = self.convw[d][:, ct * DC + j:
                                                   ct * DC + j + 1]
                                v.scalar_tensor_tensor(
                                    self.convt[j % 2][:, :], xr[:, j:j + TC],
                                    wj, self.convt[(j + 1) % 2][:, :],
                                    ALU.mult, ALU.add)
                            return v.tensor_copy(
                                self.bound[d][:, ct * 4:ct * 4 + 3],
                                xr[:, TC:TC + 3])
                        E("vector", vconv, waits=[free.get("convt")], inc="v")
                        conv_ev = em.now("v")
                        free["convt"] = conv_ev
                        free[f"xcpre{ct % 2}"] = conv_ev
                        xct_ = self.xcring[ct % 2]
                        E("scalar", lambda s, xct_=xct_:
                          s.activation(xct_[:, :], self.convt[1][:, :],
                                       AF.Silu),
                          waits=[conv_ev, free.get(f"xcr{ct % 2}")], inc="s")
                        silu_ev = em.now("s")
                        free["convt"] = silu_ev
                        E("vector", lambda v, xct_=xct_, ct=ct:
                          v.tensor_copy(self.xcbf[:, ct * TC:(ct + 1) * TC],
                                        xct_[:, :]),
                          waits=[silu_ev, free.get("xcbf_cons"),
                                 free.get("bca")], inc="v")
                        ev[f"xcbf{ct}"] = em.now("v")
                        if ch == 0 and ct == 0:
                            self.tap(f"xc0_{d}", xct_[:, :], silu_ev,
                                     guard=f"xcr{ct % 2}")
                        e = dma("gpsimd",
                                self.dram[f"s_xc_{d}"][:, ct * TOK + ch * TC:
                                                       ct * TOK + (ch + 1) * TC],
                                xct_[:, :], waits=[silu_ev],
                                sem=f"sp_xc{ct % 2}")
                        free[f"xcr{ct % 2}"] = e
                    else:
                        szt = self.szring[ct % 2]
                        E("scalar", lambda s, szt=szt, bk=bk:
                          s.activation(szt[:, :], psum[bk][:, :TC], AF.Silu),
                          waits=[bank_w[bk], free.get(f"szr{ct % 2}")],
                          inc="s")
                        bank_r[bk] = em.now("s")
                        e = dma("gpsimd",
                                self.dram[f"s_sz_{d}"][:, ct * TOK + ch * TC:
                                                       ct * TOK + (ch + 1) * TC],
                                szt[:, :], waits=[em.now("s")],
                                sem=f"sp_sz{ct % 2}")
                        free[f"szr{ct % 2}"] = e

            def mmxp(t, d=d):
                r = None
                for kc in range(NCT):
                    r = t.matmul(psum[4][0:DTR, :TC],
                                 self.xpw[d][:, kc * 96:kc * 96 + 64],
                                 self.xcbf[:, kc * TC:(kc + 1) * TC],
                                 start=(kc == 0), stop=(kc == NCT - 1))
                for kc in range(NCT):
                    r = t.matmul(psum[5][0:16, :TC],
                                 self.xpw[d][:, kc * 96 + 64:kc * 96 + 80],
                                 self.xcbf[:, kc * TC:(kc + 1) * TC],
                                 start=(kc == 0), stop=(kc == NCT - 1))
                for kc in range(NCT):
                    r = t.matmul(psum[6][0:16, :TC],
                                 self.xpw[d][:, kc * 96 + 80:kc * 96 + 96],
                                 self.xcbf[:, kc * TC:(kc + 1) * TC],
                                 start=(kc == 0), stop=(kc == NCT - 1))
                return r
            E("tensor", mmxp,
              waits=[ev[f"xcbf{c}"] for c in range(NCT)] +
                    [bank_r[4], bank_r[5], bank_r[6]], inc="t")
            bank_w[4] = bank_w[5] = bank_w[6] = em.now("t")
            free["xcbf_cons"] = bank_w[4]

            E("scalar", lambda s: s.activation(self.dtbf[:, :],
                                               psum[4][0:DTR, :TC], AF.Copy),
              waits=[bank_w[4], free.get("dtbf"), ev.get("dtspill")], inc="s")
            bank_r[4] = em.now("s")
            dt_ev = em.now("s")
            if ch == 0:
                self.tap(f"dt0_{d}", self.dtbf[:, :], dt_ev, guard="dtbf")
            ev["dtspill"] = dma("gpsimd",
                self.dram[f"s_dt_{d}"][:, ch * TC:(ch + 1) * TC],
                self.dtbf[:, :], waits=[dt_ev], sem="sp_dt")

            def vbc(v):
                v.tensor_copy(self.bhi[:, :], psum[5][0:16, :TC])
                v.tensor_sub(self.bclo[:, :], psum[5][0:16, :TC],
                             self.bhi[:, :])
                v.tensor_copy(self.blo[:, :], self.bclo[:, :])
                v.tensor_copy(self.chi[:, :], psum[6][0:16, :TC])
                v.tensor_sub(self.bclo[:, :], psum[6][0:16, :TC],
                             self.chi[:, :])
                return v.tensor_copy(self.clo[:, :], self.bclo[:, :])
            E("vector", vbc,
              waits=[bank_w[5], bank_w[6], free.get("bchl"),
                     em.now("sp_bc") if "sp_bc" in em.count else None],
              inc="v")
            hl_ev = em.now("v")
            bank_r[5] = hl_ev
            bank_r[6] = hl_ev
            ebc1 = dma("gpsimd",
                self.dram[f"s_bc_{d}"][:, 2 * ch * TC:(2 * ch + 1) * TC],
                self.chi[:, :], waits=[hl_ev], sem="sp_bc")
            dma("gpsimd",
                self.dram[f"s_bc_{d}"][:, (2 * ch + 1) * TC:(2 * ch + 2) * TC],
                self.clo[:, :], waits=[hl_ev, ebc1], sem="sp_bc")

            for grp, (shi, slo) in enumerate(((self.bhi, self.blo),
                                              (self.chi, self.clo))):
                for s_ in range(DS):
                    seg = self.bca[:, (grp * DS + s_) * TC:
                                   (grp * DS + s_ + 1) * TC]
                    bbk = 5 if s_ % 2 == 0 else 6

                    def mmbc(t, s_=s_, shi=shi, slo=slo, bbk=bbk):
                        t.matmul(psum[bbk][:, :TC],
                                 self.selw[0:16, s_ * 128:(s_ + 1) * 128],
                                 shi[:, :], start=True, stop=False)
                        return t.matmul(psum[bbk][:, :TC],
                                        self.selw[0:16, s_ * 128:(s_ + 1) * 128],
                                        slo[:, :], start=False, stop=True)
                    E("tensor", mmbc,
                      waits=[hl_ev, bank_r[bbk],
                             free.get("bca") if (grp == 0 and s_ == 0)
                             else None], inc="t")
                    bank_w[bbk] = em.now("t")
                    E("scalar", lambda s, seg=seg, bbk=bbk:
                      s.activation(seg, psum[bbk][:, :TC], AF.Copy),
                      waits=[bank_w[bbk]], inc="s")
                    bank_r[bbk] = em.now("s")
            bca_ev = em.now("s")
            if ch == 0:
                self.tap(f"bca_{d}", self.bca[:, :], bca_ev, guard="bca")

            for ct in range(NCT):
                dl = self.dlt[ct % 2]
                dbk = 4 if ct % 2 == 0 else 7

                def mmdtp(t, ct=ct, d=d, dbk=dbk):
                    return t.matmul(psum[dbk][:, :TC],
                                    self.dtpw[d][:, ct * 128:(ct + 1) * 128],
                                    self.dtbf[:, :], start=True, stop=True)
                E("tensor", mmdtp, waits=[dt_ev, bank_r[dbk],
                                          free.get(f"dlt{ct % 2}")], inc="t")
                bank_w[dbk] = em.now("t")
                sp_ev = self.softplus(dbk, dl[:, :],
                                      self.sv(f"dtp_b_{d}")[:, ct:ct + 1])

                def vsd(v, dl=dl, ct=ct, d=d):
                    v.reduce_sum(self.red1[:, :], dl[:, :], AX.X)
                    return v.tensor_add(self.sumdl[d][:, ct:ct + 1],
                                        self.sumdl[d][:, ct:ct + 1],
                                        self.red1[:, :])
                E("vector", vsd, inc="v")
                er = dma("sync", self.xcrel[ct % 2][:, :],
                         self.dram[f"s_xc_{d}"][:, ct * TOK + ch * TC:
                                                ct * TOK + (ch + 1) * TC],
                         waits=[free.get(f"xcl{ct % 2}"),
                                em.now(f"sp_xc{ct % 2}")],
                         sem=f"ld_xc{ct % 2}")
                esz = dma("sync", self.szrel[ct % 2][:, :],
                          self.dram[f"s_sz_{d}"][:, ct * TOK + ch * TC:
                                                 ct * TOK + (ch + 1) * TC],
                          waits=[free.get(f"szl{ct % 2}"),
                                 em.now(f"sp_sz{ct % 2}")],
                          sem=f"ld_sz{ct % 2}")
                E("vector", lambda v, ct=ct, dl=dl:
                  v.tensor_mul(self.dut[:, :], dl[:, :],
                               self.xcrel[ct % 2][:, :]),
                  waits=[er, free.get("dut")], inc="v")
                du_ev = em.now("v")
                if ch == 0 and ct == 0:
                    self.tap(f"dlt0_{d}", dl[:, :], du_ev, guard=f"dlt{ct % 2}")
                    self.tap(f"dut0_{d}", self.dut[:, :], du_ev, guard="dut")

                for sg in range(NSG):
                    ara = self.ar_a2[sg % 2]

                    def sda(s, ct=ct, sg=sg, dl=dl, d=d, ara=ara):
                        r = None
                        for sl_ in range(SG):
                            s_ = sg * SG + sl_
                            r = s.activation(
                                ara[:, sl_ * TC:(sl_ + 1) * TC],
                                dl[:, :], AF.Exp,
                                scale=self.Asb[d][:, ct * DS + s_:
                                                  ct * DS + s_ + 1])
                        return r
                    E("scalar", sda, waits=[sp_ev, free.get(f"ar_a{sg % 2}"),
                                            ev[f"A_{d}"]], inc="s")
                    da_ev = em.now("s")

                    def vb(v, ct=ct, sg=sg, d=d, ara=ara):
                        for sl_ in range(SG):
                            s_ = sg * SG + sl_
                            v.tensor_mul(self.ar_b[:, sl_ * TC:(sl_ + 1) * TC],
                                         self.dut[:, :],
                                         self.bca[:, s_ * TC:(s_ + 1) * TC])
                        a3 = ara.rearrange("p (s t) -> p s t", s=SG)
                        b3 = self.ar_b.rearrange("p (s t) -> p s t", s=SG)
                        hs = self.hstate[d][:, ct * DS + sg * SG:
                                            ct * DS + (sg + 1) * SG]
                        v.tensor_mul(self.mtt[0][:, 0:SG], a3[:, :, 0], hs)
                        v.tensor_add(b3[:, :, 0], b3[:, :, 0],
                                     self.mtt[0][:, 0:SG])
                        return v.memset(a3[:, :, 0:1], 0.0)
                    E("vector", vb, waits=[da_ev, du_ev, bca_ev,
                                           free.get("ar_b")], inc="v")

                    E("vector", lambda v, ara=ara: v.tensor_tensor_scan(
                        self.ar_h[:, :], ara[:, :], self.ar_b[:, :],
                        0.0, ALU.mult, ALU.add),
                      waits=[free.get("ar_h")], inc="v")
                    scan_ev = em.now("v")
                    if ch == 0 and ct == 0 and sg == 0:
                        self.tap(f"arh0_{d}", self.ar_h[:, :], scan_ev,
                                 guard="ar_h")
                    free[f"ar_a{sg % 2}"] = scan_ev
                    free["ar_b"] = scan_ev

                    E("vector", lambda v, ct=ct, sg=sg, d=d: v.tensor_copy(
                        self.hstate[d][:, ct * DS + sg * SG:
                                       ct * DS + (sg + 1) * SG],
                        self.ar_h.rearrange("p (s t) -> p s t",
                                            s=SG)[:, :, TC - 1]),
                      inc="v")

                    def vy(v, sg=sg):
                        v.tensor_mul(self.ar_m[:, :], self.ar_h[:, :],
                                     self.bca[:, (DS + sg * SG) * TC:
                                              (DS + (sg + 1) * SG) * TC])
                        v.tensor_add(self.fold[:, :],
                                     self.ar_m[:, 0:2 * TC],
                                     self.ar_m[:, 2 * TC:4 * TC])
                        if sg == 0:
                            return v.tensor_add(self.yacc[0][:, :],
                                                self.fold[:, 0:TC],
                                                self.fold[:, TC:2 * TC])
                        v.tensor_add(self.mtt[1][:, :], self.fold[:, 0:TC],
                                     self.fold[:, TC:2 * TC])
                        return v.tensor_add(self.yacc[sg % 2][:, :],
                                            self.yacc[(sg + 1) % 2][:, :],
                                            self.mtt[1][:, :])
                    E("vector", vy, inc="v")
                    free["ar_h"] = em.now("v")
                free["bca"] = em.now("v")
                free["dut"] = em.now("v")
                free[f"dlt{ct % 2}"] = em.now("v")

                yfin = self.yacc[(NSG - 1) % 2]

                def vgate(v, ct=ct, d=d, yfin=yfin):
                    v.scalar_tensor_tensor(self.mtt[0][:, :],
                                           self.xcrel[ct % 2][:, :],
                                           self.sv(f"Dk_{d}")[:, ct:ct + 1],
                                           yfin[:, :], ALU.mult, ALU.add)
                    return v.tensor_mul(self.g1ch[:, ct * TC:(ct + 1) * TC],
                                        self.mtt[0][:, :],
                                        self.szrel[ct % 2][:, :])
                E("vector", vgate, waits=[esz, free.get("g1_cons")], inc="v")
                gate_ev = em.now("v")
                if ch == 0 and ct == 0:
                    self.tap(f"y0_{d}", yfin[:, :], gate_ev, guard="yacc")
                free[f"xcl{ct % 2}"] = gate_ev
                free[f"szl{ct % 2}"] = gate_ev
                ev[f"g1_{ct}"] = gate_ev

            for mt in range(KD):
                wbuf, wb16 = self.wst32[mt % 2], self.wst16[mt % 2]
                e = dma("sync",
                        wbuf[:, :NCT * 128].rearrange("p (kc m) -> p kc m",
                                                      kc=NCT),
                        outw[:, :, mt * 128:(mt + 1) * 128],
                        waits=[free.get(f"wst{mt % 2}")] +
                              ([free.get("bca")] if mt % 2 == 0 else []),
                        sem=f"ld_w{mt % 2}")
                E("vector", lambda v, wb16=wb16, wbuf=wbuf:
                  v.tensor_copy(wb16[:, :NCT * 128], wbuf[:, :NCT * 128]),
                  waits=[e, free.get(f"wst16_{mt % 2}")], inc="v")
                cast_ev = em.now("v")
                free[f"wst{mt % 2}"] = cast_ev
                bk = mt % 2

                def mmop(t, wb16=wb16, bk=bk):
                    r = None
                    for kc in range(NCT):
                        r = t.matmul(psum[bk][:, :TC],
                                     wb16[:, kc * 128:(kc + 1) * 128],
                                     self.g1ch[:, kc * TC:(kc + 1) * TC],
                                     start=(kc == 0), stop=(kc == NCT - 1))
                    return r
                E("tensor", mmop,
                  waits=[cast_ev, bank_r[bk]] +
                        [ev[f"g1_{c}"] for c in range(NCT)], inc="t")
                bank_w[bk] = em.now("t")
                free["g1_cons"] = bank_w[bk]
                free[f"wst16_{mt % 2}"] = bank_w[bk]
                yp = self.ypst[mt % 2]
                E("scalar", lambda s, yp=yp, bk=bk:
                  s.activation(yp[:, :], psum[bk][:, :TC], AF.Copy),
                  waits=[bank_w[bk], free.get(f"ypst{mt % 2}")], inc="s")
                bank_r[bk] = em.now("s")
                e = dma("gpsimd",
                        self.dram[f"s_yp_{d}"][:, mt * TOK + ch * TC:
                                               mt * TOK + (ch + 1) * TC],
                        yp[:, :], waits=[em.now("s")],
                        sem=f"sp_yp{mt % 2}")
                free[f"ypst{mt % 2}"] = e
            if ch == 0:
                self.tap(f"ypch0_{d}", self.ypst[(KD - 1) % 2][:, :],
                         em.now("s"), guard=f"ypst{(KD - 1) % 2}")
            free["bchl"] = em.now("s")
            free["dtbf"] = em.now("v")

        if _TRUNC:
            return

        def vtk(v, d=d):
            last = None
            for ct in range(NCT):
                last = v.tensor_scalar(self.tkt[:, ct * DS:(ct + 1) * DS],
                                       self.Asb[d][:, ct * DS:(ct + 1) * DS],
                                       self.sumdl[d][:, ct:ct + 1], None,
                                       ALU.mult)
            return last
        E("vector", vtk, inc="v")
        E("scalar", lambda s: s.activation(self.tkt[:, :], self.tkt[:, :],
                                           AF.Exp),
          waits=[em.now("v")], inc="s")
        tk_ev = em.now("s")

        def vstage(v, d=d):
            v.tensor_copy(self.cctmp[:, 0:NCT * DS], self.tkt[:, :])
            return v.tensor_copy(self.cctmp[:, NCT * DS:], self.hstate[d][:, :])
        E("vector", vstage, waits=[tk_ev, free.get("cctmp")], inc="v")
        e = dma("gpsimd", self.dram[f"cc_in_{d}"][:, :], self.cctmp[:, :],
                waits=[em.now("v")], sem="sp_cc")
        ev[f"ccin_{d}"] = e
        free["cctmp"] = e
        E("gpsimd", lambda g, d=d: g.collective_compute(
            "AllGather", ALU.bypass,
            replica_groups=[[0, 1, 2, 3], [4, 5, 6, 7]],
            ins=[self.dram[f"cc_in_{d}"][:, :].opt()],
            outs=[self.dram[f"cc_out_{d}"][:, :].opt()]),
          waits=[e], inc="cc")
        ev[f"cc_{d}"] = em.now("cc")

    def emit_xnsum_cc(self):
        E, dma, em, ev, free = self.E, self.dma, self.em, self.ev, self.free
        E("vector", lambda v: v.tensor_copy(self.cctmp[:, 0:KD],
                                            self.xnsum[:, :]),
          waits=[free.get("cctmp")], inc="v")
        e = dma("gpsimd", self.dram["cc_in_m"][:, :], self.cctmp[:, 0:KD],
                waits=[em.now("v")], sem="sp_cc")
        free["cctmp"] = e
        E("gpsimd", lambda g: g.collective_compute(
            "AllReduce", ALU.add, replica_groups=[[0, 1, 2, 3], [4, 5, 6, 7]],
            ins=[self.dram["cc_in_m"][:, :].opt()],
            outs=[self.dram["cc_out_m"][:, :].opt()]),
          waits=[e], inc="cc")
        ev["cc_m"] = em.now("cc")

    def emit_ffn(self):
        E, dma, em, ev, free = self.E, self.dma, self.em, self.ev, self.free
        psum, bank_w, bank_r = self.psum, self.bank_w, self.bank_r
        e = dma("sync", self.cctmp[:, 0:KD], self.dram["cc_out_m"][:, :],
                waits=[ev["cc_m"], free.get("cctmp")], sem="ld_cc")
        E("vector", lambda v: v.tensor_scalar(self.cctmp[:, KD:2 * KD],
                                              self.cctmp[:, 0:KD], 1.0 / L,
                                              None, ALU.mult),
          waits=[e], inc="v")
        E("vector", lambda v: v.tensor_copy(self.pb1[:, 0:KD],
                                            self.cctmp[:, KD:2 * KD]),
          waits=[free.get("pb1")], inc="v")
        mean_ev = em.now("v")
        free["cctmp"] = mean_ev

        eg1 = dma("sync", self.wstB[:, :KD * 256],
                  self.dram["gc_w1"].rearrange("p (kc m) -> p kc m", kc=KD)
                  [:, :, 0:256], waits=[free.get("wst1")], sem="ld_w1")
        eg1b = dma("sync", self.wstA[:, :KD * 256],
                   self.dram["gc_w1"].rearrange("p (kc m) -> p kc m", kc=KD)
                   [:, :, 256:512],
                   waits=[free.get("wst0"), free.get("bca")], sem="ld_w0")
        E("vector", lambda v: v.tensor_copy(self.wst16[0][:, :],
                                            self.wstB[:, :]),
          waits=[eg1, free.get("wst16_0")], inc="v")
        free["wst1"] = em.now("v")
        E("vector", lambda v: v.tensor_copy(self.wst16[1][:, :],
                                            self.wstA[:, :]),
          waits=[eg1b, free.get("wst16_1")], inc="v")
        free["wst0"] = em.now("v")
        g1c = em.now("v")

        def mmgf1(t):
            r = None
            for mt in range(4):
                wb = self.wst16[0] if mt < 2 else self.wst16[1]
                mo = mt % 2
                for kc in range(KD):
                    r = t.matmul(psum[2][:, mt:mt + 1],
                                 wb[:, kc * 256 + mo * 128:
                                    kc * 256 + (mo + 1) * 128],
                                 self.pb1[:, kc:kc + 1],
                                 start=(kc == 0), stop=(kc == KD - 1))
            return r
        E("tensor", mmgf1, waits=[g1c, mean_ev, bank_r[2]], inc="t")
        bank_w[2] = em.now("t")
        free["wst16_0"] = bank_w[2]
        free["wst16_1"] = bank_w[2]
        def sgf1(s):
            r = None
            for mt in range(4):
                r = s.activation(self.px1[:, mt:mt + 1], psum[2][:, mt:mt + 1],
                                 AF.Gelu, bias=self.sv("gc_b1")[:, mt:mt + 1])
            return r
        E("scalar", sgf1, waits=[bank_w[2], free.get("px1")], inc="s")
        bank_r[2] = em.now("s")
        E("vector", lambda v: v.tensor_copy(self.pb1[:, KD:KD + 4],
                                            self.px1[:, 0:4]),
          waits=[em.now("s")], inc="v")
        h1_ev = em.now("v")
        eg2 = dma("sync", self.wstB[:, :4 * 256], self.dram["gc_w2"][:, :],
                  waits=[free.get("wst1")], sem="ld_w1")
        E("vector", lambda v: v.tensor_copy(self.wst16[0][:, :4 * 256],
                                            self.wstB[:, :4 * 256]),
          waits=[eg2, free.get("wst16_0")], inc="v")
        free["wst1"] = em.now("v")
        g2c = em.now("v")

        def mmgf2(t):
            r = None
            for mt in range(2):
                for kc in range(4):
                    r = t.matmul(psum[3][:, mt:mt + 1],
                                 self.wst16[0][:, kc * 256 + mt * 128:
                                               kc * 256 + (mt + 1) * 128],
                                 self.pb1[:, KD + kc:KD + kc + 1],
                                 start=(kc == 0), stop=(kc == 3))
            return r
        E("tensor", mmgf2, waits=[g2c, h1_ev, bank_r[3]], inc="t")
        bank_w[3] = em.now("t")
        free["wst16_0"] = bank_w[3]
        def sgf2(s):
            r = None
            for mt in range(2):
                r = s.activation(self.px1[:, 4 + mt:5 + mt],
                                 psum[3][:, mt:mt + 1], AF.Gelu,
                                 bias=self.sv("gc_b2")[:, mt:mt + 1])
            return r
        E("scalar", sgf2, waits=[bank_w[3]], inc="s")
        bank_r[3] = em.now("s")
        E("vector", lambda v: v.tensor_copy(self.pb1[:, KD + 4:KD + 6],
                                            self.px1[:, 4:6]),
          waits=[em.now("s")], inc="v")
        h2_ev = em.now("v")
        eg3 = dma("sync", self.wstB[:, :4], self.dram["gc_w3"][:, :],
                  waits=[free.get("wst1")], sem="ld_w1")
        E("vector", lambda v: v.tensor_copy(self.wst16[0][:, :4],
                                            self.wstB[:, :4]),
          waits=[eg3, free.get("wst16_0")], inc="v")
        free["wst1"] = em.now("v")
        g3c = em.now("v")

        def mmgf3(t):
            r = None
            for kc in range(2):
                r = t.matmul(psum[2][0:2, 0:1],
                             self.wst16[0][:, kc * 2:kc * 2 + 2],
                             self.pb1[:, KD + 4 + kc:KD + 5 + kc],
                             start=(kc == 0), stop=(kc == 1))
            return r
        E("tensor", mmgf3, waits=[g3c, h2_ev, bank_r[2]], inc="t")
        bank_w[2] = em.now("t")
        free["wst16_0"] = bank_w[2]
        E("scalar", lambda s: s.activation(self.gfrow[:, 3:4],
                                           psum[2][0:2, 0:1], AF.Sigmoid,
                                           bias=self.sv("gc_b3")[0:2, :]),
          waits=[bank_w[2]], inc="s")
        bank_r[2] = em.now("s")

        def vgf(v):
            v.tensor_scalar(self.gfrow[:, 0:1], self.gfrow[:, 3:4], 0.7, None,
                            ALU.mult)
            v.tensor_scalar(self.gfrow[:, 1:2], self.sv("ar")[0:2, :], 2.0, None,
                            ALU.mult)
            return v.tensor_scalar(self.gfrow[:, 2:3], self.sv("ar")[0:2, :], -1.0,
                                   0.1, ALU.mult, ALU.add)
        E("vector", vgf, waits=[em.now("s")], inc="v")
        gf_ev = em.now("v")

        cn1bf = self.g1ch[:, 0:KD * D // 2]
        cn1bf2 = self.bca[:, 0:KD * D // 2]
        cn2bf = self.pb1[:, 0:KD * 512 // 2]
        cn3bf = self.xcbf[:, 0:4 * 256]
        cn4bf = self.xcbf[:, 4 * 256:4 * 256 + 4]
        e3 = dma("sync", self.wstB[:, :4 * 256], self.dram["cn_w3"][:, :],
                 waits=[free.get("wst1")], sem="ld_w1")
        E("vector", lambda v: v.tensor_copy(cn3bf, self.wstB[:, :4 * 256]),
          waits=[e3, free.get("xcbf_cons"), free.get("stats_sq")], inc="v")
        free["wst1"] = em.now("v")
        e4 = dma("sync", self.wstB[:, 4 * 256:4 * 256 + 4],
                 self.dram["cn_w4"][:, :], waits=[e3], sem="ld_w1")
        E("vector", lambda v: v.tensor_copy(cn4bf,
                                            self.wstB[:, 4 * 256:4 * 256 + 4]),
          waits=[e4], inc="v")
        cn_ev = em.now("v")
        free["wst1"] = cn_ev
        free["stats_sq"] = None

        x3 = self.dram["x_f"].rearrange("p (kc t) -> p kc t", kc=KD)
        cn1w = self.dram["cn_w1"].rearrange("p (kc m) -> p kc m", kc=KD)
        cn2w = self.dram["cn_w2"].rearrange("p (kc m) -> p kc m", kc=KD)
        outc = self.px1
        h1f = self.px2
        h1b = self.xn2bf
        outb = self.xbf
        for ch in range(NCH):
            exld = dma("sync", self.xch.rearrange("p (kc t) -> p kc t", kc=KD),
                       x3[:, :, 3 + ch * TC:3 + (ch + 1) * TC],
                       waits=[free.get("xch")], sem="ld_x")
            free["ln_dst"] = free.get("outc_cons")
            free["stats_xb"] = free.get("outb_cons")
            ln_ev = self.emit_ln(self.xch[:, :], TC, outc[:, :], None,
                                 self.lnr, self.lnhl, do_sum=False, sA=2,
                                 sB=3, xb=self.xbf[:, :], sq=self.ffnsq,
                                 src_ev=exld)
            free["xch"] = ln_ev
            prev_ev = ln_ev
            for cy in range(3):
                E("vector", lambda v: v.tensor_copy(outb[:, :], outc[:, :]),
                  waits=[prev_ev, free.get("outb_cons")], inc="v")
                ob_ev = em.now("v")
                for mbk in range(4):
                    wbuf = self.wst32[mbk % 2]
                    wb16 = self.wst16[mbk % 2]
                    e = dma("sync", wbuf.rearrange("p (kc m) -> p kc m", kc=KD),
                            cn1w[:, :, mbk * 256:(mbk + 1) * 256],
                            waits=[free.get(f"wst{mbk % 2}")],
                            sem=f"ld_w{mbk % 2}")
                    E("vector", lambda v, wb16=wb16, wbuf=wbuf:
                      v.tensor_copy(wb16[:, :], wbuf[:, :]),
                      waits=[e, free.get(f"wst16_{mbk % 2}")], inc="v")
                    cast_ev = em.now("v")
                    free[f"wst{mbk % 2}"] = cast_ev
                    for sub in range(2):
                        mt = mbk * 2 + sub
                        bk = mt % 2

                        def mmh1(t, wb16=wb16, sub=sub, bk=bk):
                            r = None
                            for kc in range(KD):
                                lh = wb16[:, kc * 256 + sub * 128:
                                          kc * 256 + (sub + 1) * 128]
                                r = t.matmul(psum[bk][:, :TC], lh,
                                             outb[:, kc * TC:(kc + 1) * TC],
                                             start=(kc == 0),
                                             stop=(kc == KD - 1))
                            return r
                        E("tensor", mmh1, waits=[ob_ev, cast_ev, bank_r[bk]],
                          inc="t")
                        bank_w[bk] = em.now("t")
                        free[f"wst16_{mbk % 2}"] = bank_w[bk]
                        free["outb_cons"] = bank_w[bk]
                        E("scalar", lambda s, mt=mt, bk=bk:
                          s.activation(h1f[:, mt * TC:(mt + 1) * TC],
                                       psum[bk][:, :TC], AF.Gelu,
                                       bias=self.sv("cn_b1")[:, mt:mt + 1]),
                          waits=[bank_w[bk], free.get("h1f")], inc="s")
                        bank_r[bk] = em.now("s")
                E("vector", lambda v: v.tensor_copy(h1b[:, :], h1f[:, :]),
                  waits=[em.now("s"), free.get("h1b_cons")], inc="v")
                h1b_ev = em.now("v")
                for mbk in range(2):
                    wbuf = self.wst32[mbk % 2]
                    wb16 = self.wst16[mbk % 2]
                    e = dma("sync", wbuf.rearrange("p (kc m) -> p kc m", kc=KD),
                            cn2w[:, :, mbk * 256:(mbk + 1) * 256],
                            waits=[free.get(f"wst{mbk % 2}")],
                            sem=f"ld_w{mbk % 2}")
                    E("vector", lambda v, wb16=wb16, wbuf=wbuf:
                      v.tensor_copy(wb16[:, :], wbuf[:, :]),
                      waits=[e, free.get(f"wst16_{mbk % 2}")], inc="v")
                    cast_ev = em.now("v")
                    free[f"wst{mbk % 2}"] = cast_ev
                    for sub in range(2):
                        mt = mbk * 2 + sub
                        bk = mt % 2

                        def mmh2(t, wb16=wb16, sub=sub, bk=bk):
                            r = None
                            for kc in range(KD):
                                lh = wb16[:, kc * 256 + sub * 128:
                                          kc * 256 + (sub + 1) * 128]
                                r = t.matmul(psum[bk][:, :TC], lh,
                                             h1b[:, kc * TC:(kc + 1) * TC],
                                             start=(kc == 0),
                                             stop=(kc == KD - 1))
                            return r
                        E("tensor", mmh2, waits=[h1b_ev, cast_ev, bank_r[bk]],
                          inc="t")
                        bank_w[bk] = em.now("t")
                        free[f"wst16_{mbk % 2}"] = bank_w[bk]
                        free["h1b_cons"] = bank_w[bk]
                        E("scalar", lambda s, mt=mt, bk=bk:
                          s.activation(h1f[:, mt * TC:(mt + 1) * TC],
                                       psum[bk][:, :TC], AF.Gelu,
                                       bias=self.sv("cn_b2")[:, mt:mt + 1]),
                          waits=[bank_w[bk]], inc="s")
                        bank_r[bk] = em.now("s")
                free["h1f"] = em.now("s")
                E("vector", lambda v: v.tensor_copy(h1b[:, 0:4 * TC],
                                                    h1f[:, 0:4 * TC]),
                  waits=[em.now("s")], inc="v")
                h2b_ev = em.now("v")
                free["h1b_cons"] = None
                for mt in range(2):
                    bk = mt % 2

                    def mmh3(t, mt=mt, bk=bk):
                        r = None
                        for kc in range(4):
                            r = t.matmul(psum[bk][:, :TC],
                                         cn3bf[:, kc * 256 + mt * 128:
                                               kc * 256 + (mt + 1) * 128],
                                         h1b[:, kc * TC:(kc + 1) * TC],
                                         start=(kc == 0), stop=(kc == 3))
                        return r
                    E("tensor", mmh3, waits=[h2b_ev, cn_ev, bank_r[bk]],
                      inc="t")
                    bank_w[bk] = em.now("t")
                    free["h1b_cons"] = bank_w[bk]
                    E("scalar", lambda s, mt=mt, bk=bk:
                      s.activation(h1f[:, mt * TC:(mt + 1) * TC],
                                   psum[bk][:, :TC], AF.Gelu,
                                   bias=self.sv("cn_b3")[:, mt:mt + 1]),
                      waits=[bank_w[bk], free.get("h1f")], inc="s")
                    bank_r[bk] = em.now("s")
                free["h1f"] = em.now("s")
                E("vector", lambda v: v.tensor_copy(h1b[:, 0:2 * TC],
                                                    h1f[:, 0:2 * TC]),
                  waits=[em.now("s"), free.get("h1b_cons")], inc="v")
                h3b_ev = em.now("v")
                free["h1b_cons"] = None

                def mmtf(t):
                    r = None
                    for kc in range(2):
                        r = t.matmul(psum[2][0:2, :TC],
                                     cn4bf[:, kc * 2:kc * 2 + 2],
                                     h1b[:, kc * TC:(kc + 1) * TC],
                                     start=(kc == 0), stop=(kc == 1))
                    return r
                E("tensor", mmtf, waits=[h3b_ev, cn_ev, bank_r[2]], inc="t")
                bank_w[2] = em.now("t")
                free["h1b_cons"] = bank_w[2]
                E("scalar", lambda s: s.activation(self.lnr[0][:, :],
                                                   psum[2][0:2, :TC],
                                                   AF.Sigmoid,
                                                   bias=self.sv("cn_b4")[0:2, :]),
                  waits=[bank_w[2]], inc="s")
                bank_r[2] = em.now("s")

                def vcp(v):
                    v.tensor_scalar(self.lnr[1][:, :], self.lnr[0][:, :], 0.3,
                                    self.gfrow[:, 0:1], ALU.mult, ALU.add)
                    v.tensor_scalar(self.lnr[1][:, :], self.lnr[1][:, :],
                                    self.gfrow[:, 1:2], self.gfrow[:, 2:3],
                                    ALU.mult, ALU.add)
                    v.tensor_copy(self.lnhl[:, :], self.lnr[1][:, :])
                    v.tensor_sub(self.lnr[2][:, :], self.lnr[1][:, :],
                                 self.lnhl[:, :])
                    return v.tensor_copy(self.lnlo[:, :], self.lnr[2][:, :])
                E("vector", vcp, waits=[em.now("s"), gf_ev], inc="v")
                def mmcpl(t):
                    t.matmul(psum[3][:, :TC], self.ones2[0:1, :],
                             self.lnhl[0:1, :], start=True, stop=False)
                    return t.matmul(psum[3][:, :TC], self.ones2[0:1, :],
                                    self.lnlo[0:1, :], start=False, stop=True)
                E("tensor", mmcpl, waits=[em.now("v"), bank_r[3]], inc="t")
                bank_w[3] = em.now("t")

                def vtw(v):
                    last = None
                    for qa, qb, sgn in ((0, 2, +1), (1, 3, -1)):
                        for k in range(2):
                            a = outc[:, (2 * qa + k) * TC:
                                     (2 * qa + k + 1) * TC]
                            b = outc[:, (2 * qb + k) * TC:
                                     (2 * qb + k + 1) * TC]
                            ta = self.mtt[0][:, :]
                            tb = self.mtt[1][:, :]
                            v.tensor_mul(ta, psum[3][:, :TC], b)
                            v.tensor_mul(tb, psum[3][:, :TC], a)
                            if sgn > 0:
                                v.tensor_add(a, a, ta)
                                last = v.tensor_sub(b, b, tb)
                            else:
                                v.tensor_sub(a, a, ta)
                                last = v.tensor_add(b, b, tb)
                    return last
                E("vector", vtw, waits=[bank_w[3], free.get("mtt0"),
                                        free.get("mtt1")], inc="v")
                bank_r[3] = em.now("v")
                prev_ev = em.now("v")
            e = dma("gpsimd", self.dram["s_ffn"].rearrange(
                "p (kc t) -> p kc t", kc=KD)[:, :, ch * TC:(ch + 1) * TC],
                outc.rearrange("p (kc t) -> p kc t", kc=KD),
                waits=[prev_ev], sem="sp_ffn")
            free["outc_cons"] = e
        free["px1"] = e
        free["pb1"] = em.now("t")
        free["ypb"] = em.now("t")
        free["stats_xb"] = em.now("t")
        free["outb_cons"] = None
        free["xcbf_cons"] = em.now("t")

    def emit_combine(self, d):
        E, dma, em, ev, free = self.E, self.dma, self.em, self.ev, self.free
        order = (1, 2, 3) if d == "f" else (2, 1, 0)
        first = 0 if d == "f" else 3
        e0 = dma("sync", self.cctmp[:, :],
                 self.dram[f"cc_out_{d}"][128 * first:128 * (first + 1), :],
                 waits=[ev[f"cc_{d}"], free.get("cctmp"),
                        ev.get("ccin_b")], sem="ld_cc")
        E("vector", lambda v: v.tensor_copy(self.hpre[0][:, :],
                                            self.cctmp[:, NCT * DS:]),
          waits=[e0], inc="v")
        oh = self.sv(f"oh_{d}")
        for i, rk in enumerate(order[:2]):
            E("vector", lambda v, i=i, oh=oh, d=d: v.scalar_tensor_tensor(
                self.hin[d][:, :], self.hpre[0][:, :], oh[:, i + 1:i + 2],
                self.hin[d][:, :], ALU.mult, ALU.add), inc="v")
            e = dma("sync", self.cctmp[:, :],
                    self.dram[f"cc_out_{d}"][128 * rk:128 * (rk + 1), :],
                    waits=[em.now("v")], sem="ld_cc")

            def vcomb(v):
                v.tensor_mul(self.hpre[1][:, :], self.cctmp[:, 0:NCT * DS],
                             self.hpre[0][:, :])
                return v.tensor_add(self.hpre[0][:, :], self.hpre[1][:, :],
                                    self.cctmp[:, NCT * DS:])
            E("vector", vcomb, waits=[e], inc="v")
        E("vector", lambda v, oh=oh, d=d: v.scalar_tensor_tensor(
            self.hin[d][:, :], self.hpre[0][:, :], oh[:, 3:4],
            self.hin[d][:, :], ALU.mult, ALU.add), inc="v")
        ev[f"hin_{d}"] = em.now("v")
        free["cctmp"] = em.now("v")
        E("vector", lambda v: v.memset(self.pcarry[:, :], 0.0), inc="v")

    def emit_corr(self, d):
        E, dma, em, ev, free = self.E, self.dma, self.em, self.ev, self.free
        psum, bank_w, bank_r = self.psum, self.bank_w, self.bank_r
        outw = self.dram[f"out_wT_{d}"].rearrange("p (kc m) -> p kc m", kc=NCT)
        for ch in range(NCH):
            edt = dma("sync", self.dtbf[:, :],
                      self.dram[f"s_dt_{d}"][:, ch * TC:(ch + 1) * TC],
                      waits=[free.get("dtbf"), em.now("sp_dt")], sem="ld_dt")
            ech = dma("sync", self.chi[:, :],
                      self.dram[f"s_bc_{d}"][:, 2 * ch * TC:(2 * ch + 1) * TC],
                      waits=[free.get("bchl"), em.now("sp_bc")], sem="ld_bc")
            ecl = dma("sync", self.clo[:, :],
                      self.dram[f"s_bc_{d}"][:, (2 * ch + 1) * TC:
                                             (2 * ch + 2) * TC],
                      waits=[free.get("bchl"), em.now("sp_bc"), ech],
                      sem="ld_bc")
            eyp = dma("sync", self.pb1.rearrange("p (kc t) -> p kc t", kc=KD),
                      self.dram[f"s_yp_{d}"].rearrange("p (kc t) -> p kc t",
                                                       kc=KD)
                      [:, :, ch * TC:(ch + 1) * TC],
                      waits=[free.get("pb1"), em.now("sp_yp0"),
                             em.now("sp_yp1")], sem="ld_yp")
            for s_ in range(DS):
                seg = self.bca[:, (DS + s_) * TC:(DS + s_ + 1) * TC]
                bbk = 5 if s_ % 2 == 0 else 6

                def mmbc(t, s_=s_, bbk=bbk):
                    t.matmul(psum[bbk][:, :TC],
                             self.selw[0:16, s_ * 128:(s_ + 1) * 128],
                             self.chi[:, :], start=True, stop=False)
                    return t.matmul(psum[bbk][:, :TC],
                                    self.selw[0:16, s_ * 128:(s_ + 1) * 128],
                                    self.clo[:, :], start=False, stop=True)
                E("tensor", mmbc,
                  waits=[ech, ecl, bank_r[bbk],
                         free.get("bca") if s_ == 0 else None], inc="t")
                bank_w[bbk] = em.now("t")
                E("scalar", lambda s, seg=seg, bbk=bbk:
                  s.activation(seg, psum[bbk][:, :TC], AF.Copy),
                  waits=[bank_w[bbk]], inc="s")
                bank_r[bbk] = em.now("s")
            bca_ev = em.now("s")

            for ct in range(NCT):
                dl = self.dlt[ct % 2]
                dbk = 4 if ct % 2 == 0 else 7

                def mmdtp(t, ct=ct, d=d, dbk=dbk):
                    return t.matmul(psum[dbk][:, :TC],
                                    self.dtpw[d][:, ct * 128:(ct + 1) * 128],
                                    self.dtbf[:, :], start=True, stop=True)
                E("tensor", mmdtp, waits=[edt, bank_r[dbk],
                                          free.get(f"dlt{ct % 2}")], inc="t")
                bank_w[dbk] = em.now("t")
                self.softplus(dbk, dl[:, :], self.sv(f"dtp_b_{d}")[:, ct:ct + 1])
                E("vector", lambda v, dl=dl, ct=ct: v.tensor_tensor_scan(
                    self.dut[:, :], dl[:, :], self.zerosT[:, :],
                    self.pcarry[:, ct:ct + 1], ALU.add, ALU.add),
                  waits=[free.get("dut")], inc="v")
                p_ev = em.now("v")
                E("vector", lambda v, ct=ct: v.tensor_copy(
                    self.pcarry[:, ct:ct + 1], self.dut[:, TC - 1:TC]),
                  inc="v")
                free[f"dlt{ct % 2}"] = em.now("v")

                acc_i = 0
                for sg in range(NSG):
                    ea = self.ett[sg % 2]

                    def se(s, ct=ct, sg=sg, d=d, ea=ea):
                        r = None
                        for sl_ in range(SG):
                            s_ = sg * SG + sl_
                            r = s.activation(
                                ea[:, sl_ * TC:(sl_ + 1) * TC],
                                self.dut[:, :], AF.Exp,
                                scale=self.Asb[d][:, ct * DS + s_:
                                                  ct * DS + s_ + 1])
                        return r
                    E("scalar", se, waits=[p_ev, free.get(f"corr_e{sg % 2}")],
                      inc="s")
                    e_ev = em.now("s")

                    def vmacc(v, sg=sg, ct=ct, d=d, ea=ea, acc_i=acc_i):
                        v.tensor_mul(self.ar_m[:, :], ea[:, :],
                                     self.bca[:, (DS + sg * SG) * TC:
                                              (DS + (sg + 1) * SG) * TC])
                        last = None
                        for sl_ in range(SG):
                            s_ = sg * SG + sl_
                            hcol = self.hin[d][:, ct * DS + s_:
                                               ct * DS + s_ + 1]
                            j = acc_i + sl_
                            if j == 0:
                                last = v.tensor_scalar(
                                    self.yacc[0][:, :],
                                    self.ar_m[:, sl_ * TC:(sl_ + 1) * TC],
                                    hcol, None, ALU.mult)
                            else:
                                last = v.scalar_tensor_tensor(
                                    self.yacc[j % 2][:, :],
                                    self.ar_m[:, sl_ * TC:(sl_ + 1) * TC],
                                    hcol, self.yacc[(j + 1) % 2][:, :],
                                    ALU.mult, ALU.add)
                        return last
                    E("vector", vmacc, waits=[e_ev, bca_ev, ev[f"hin_{d}"]],
                      inc="v")
                    free[f"corr_e{sg % 2}"] = em.now("v")
                    acc_i += SG
                free["dut"] = em.now("v")

                esz = dma("sync", self.szrel[ct % 2][:, :],
                          self.dram[f"s_sz_{d}"][:, ct * TOK + ch * TC:
                                                 ct * TOK + (ch + 1) * TC],
                          waits=[free.get(f"szl{ct % 2}"),
                                 em.now(f"sp_sz{ct % 2}")],
                          sem=f"ld_sz{ct % 2}")
                yfin = self.yacc[(DS - 1) % 2]
                E("vector", lambda v, ct=ct, yfin=yfin:
                  v.tensor_mul(self.g1ch[:, ct * TC:(ct + 1) * TC],
                               yfin[:, :], self.szrel[ct % 2][:, :]),
                  waits=[esz, free.get("g1_cons")], inc="v")
                free[f"szl{ct % 2}"] = em.now("v")
                ev[f"g1_{ct}"] = em.now("v")
            free["bca"] = em.now("v")

            for mt in range(KD):
                wbuf, wb16 = self.wst32[mt % 2], self.wst16[mt % 2]
                e = dma("sync",
                        wbuf[:, :NCT * 128].rearrange("p (kc m) -> p kc m",
                                                      kc=NCT),
                        outw[:, :, mt * 128:(mt + 1) * 128],
                        waits=[free.get(f"wst{mt % 2}")] +
                              ([free.get("bca")] if mt % 2 == 0 else []),
                        sem=f"ld_w{mt % 2}")
                E("vector", lambda v, wb16=wb16, wbuf=wbuf:
                  v.tensor_copy(wb16[:, :NCT * 128], wbuf[:, :NCT * 128]),
                  waits=[e, free.get(f"wst16_{mt % 2}")], inc="v")
                cast_ev = em.now("v")
                free[f"wst{mt % 2}"] = cast_ev
                bk = mt % 2

                def mmop(t, wb16=wb16, bk=bk):
                    r = None
                    for kc in range(NCT):
                        r = t.matmul(psum[bk][:, :TC],
                                     wb16[:, kc * 128:(kc + 1) * 128],
                                     self.g1ch[:, kc * TC:(kc + 1) * TC],
                                     start=(kc == 0), stop=(kc == NCT - 1))
                    return r
                E("tensor", mmop,
                  waits=[cast_ev, bank_r[bk]] +
                        [ev[f"g1_{c}"] for c in range(NCT)], inc="t")
                bank_w[bk] = em.now("t")
                free["g1_cons"] = bank_w[bk]
                free[f"wst16_{mt % 2}"] = bank_w[bk]
                yp = self.ypst[mt % 2]
                E("vector", lambda v, yp=yp, bk=bk, mt=mt:
                  v.tensor_add(yp[:, :], psum[bk][:, :TC],
                               self.pb1[:, mt * TC:(mt + 1) * TC]),
                  waits=[bank_w[bk], eyp, free.get(f"ypst{mt % 2}")], inc="v")
                bank_r[bk] = em.now("v")
                e = dma("gpsimd",
                        self.dram[f"s_yp_{d}"][:, mt * TOK + ch * TC:
                                               mt * TOK + (ch + 1) * TC],
                        yp[:, :], waits=[em.now("v")],
                        sem=f"sp_yp{mt % 2}")
                free[f"ypst{mt % 2}"] = e
            free["pb1"] = em.now("v")
            free["bchl"] = em.now("v")
            free["dtbf"] = em.now("v")

    def emit_final(self):
        E, dma, em, ev, free = self.E, self.dma, self.em, self.ev, self.free
        psum, bank_w, bank_r = self.psum, self.bank_w, self.bank_r
        x3 = self.dram["x_f"].rearrange("p (kc t) -> p kc t", kc=KD)
        projw = self.dram["proj_wT"].rearrange("p (kc m) -> p kc m", kc=NCT)
        ypf = self.pb1
        ypb = self.g1ch[:, 0:KD * TC]
        for ch in range(NCH):
            ef = dma("sync", ypf.rearrange("p (kc t) -> p kc t", kc=KD),
                     self.dram["s_yp_f"].rearrange("p (kc t) -> p kc t", kc=KD)
                     [:, :, ch * TC:(ch + 1) * TC],
                     waits=[free.get("pb1"), em.now("sp_yp0"),
                            em.now("sp_yp1")], sem="ld_yp")
            eb = dma("sync", ypb.rearrange("p (kc t) -> p kc t", kc=KD),
                     self.dram["s_yp_b"].rearrange("p (kc t) -> p kc t", kc=KD)
                     [:, :, TOK - (ch + 1) * TC:TOK - ch * TC],
                     waits=[free.get("g1_cons"), em.now("sp_yp0"),
                            em.now("sp_yp1")], sem="ld_ypb")
            exld = dma("sync", self.xch.rearrange("p (kc t) -> p kc t", kc=KD),
                       x3[:, :, 3 + ch * TC:3 + (ch + 1) * TC],
                       waits=[free.get("xch")], sem="ld_x")
            free["ln_dst"] = free.get("px2_cons")
            ln_ev = self.emit_ln(self.xch[:, :], TC, self.px2[:, :], None,
                                 self.lnr, self.lnhl, do_sum=False, sA=2,
                                 sB=3, xb=self.xbf[:, :], sq=self.ffnsq,
                                 src_ev=exld)
            free["xch"] = ln_ev
            eff = dma("sync", self.px1.rearrange("p (kc t) -> p kc t", kc=KD),
                      self.dram["s_ffn"].rearrange("p (kc t) -> p kc t", kc=KD)
                      [:, :, ch * TC:(ch + 1) * TC],
                      waits=[free.get("px1"), em.now("sp_ffn")], sem="ld_ffn")
            for mt in range(KD):
                wbuf, wb16 = self.wst32[mt % 2], self.wst16[mt % 2]
                e = dma("sync",
                        wbuf[:, :NCT * 128].rearrange("p (kc m) -> p kc m",
                                                      kc=NCT),
                        projw[:, :, mt * 128:(mt + 1) * 128],
                        waits=[free.get(f"wst{mt % 2}")], sem=f"ld_w{mt % 2}")
                E("vector", lambda v, wb16=wb16, wbuf=wbuf:
                  v.tensor_copy(wb16[:, :NCT * 128], wbuf[:, :NCT * 128]),
                  waits=[e, free.get(f"wst16_{mt % 2}")], inc="v")
                cast_ev = em.now("v")
                free[f"wst{mt % 2}"] = cast_ev
                bk = mt % 2

                def mmpr(t, wb16=wb16, bk=bk):
                    r = None
                    for kc in range(NCT):
                        if kc < KD:
                            rhs = ypf[:, kc * TC:(kc + 1) * TC]
                        else:
                            rhs = ypb[:, (kc - KD) * TC:
                                      (kc - KD + 1) * TC][:, ::-1]
                        r = t.matmul(psum[bk][:, :TC],
                                     wb16[:, kc * 128:(kc + 1) * 128], rhs,
                                     start=(kc == 0), stop=(kc == NCT - 1))
                    return r
                E("tensor", mmpr, waits=[cast_ev, ef, eb, bank_r[bk]], inc="t")
                bank_w[bk] = em.now("t")
                free[f"wst16_{mt % 2}"] = bank_w[bk]
                free["pb1"] = bank_w[bk]
                free["g1_cons"] = bank_w[bk]
                E("scalar", lambda s, bk=bk, mt=mt:
                  s.activation(self.mtt[0][:, :], psum[bk][:, :TC], AF.Identity,
                               bias=self.sv("proj_b")[:, mt:mt + 1]),
                  waits=[bank_w[bk], free.get("mtt0")], inc="s")
                bank_r[bk] = em.now("s")
                s_ev = em.now("s")
                E("scalar", lambda s, mt=mt:
                  s.activation(self.mtt[1][:, :],
                               self.px1[:, mt * TC:(mt + 1) * TC], AF.Silu),
                  waits=[eff, free.get("mtt1")], inc="s")
                g_ev = em.now("s")

                def vfin(v, mt=mt):
                    v.tensor_add(self.mtt[0][:, :], self.mtt[0][:, :],
                                 self.px2[:, mt * TC:(mt + 1) * TC])
                    v.tensor_mul(self.mtt[0][:, :], self.mtt[0][:, :],
                                 self.mtt[1][:, :])
                    return v.tensor_add(self.px2[:, mt * TC:(mt + 1) * TC],
                                        self.xch[:, mt * TC:(mt + 1) * TC],
                                        self.mtt[0][:, :])
                E("vector", vfin, waits=[s_ev, g_ev, ln_ev], inc="v")
                free["mtt0"] = em.now("v")
                free["mtt1"] = em.now("v")
            fin_ev = em.now("v")
            e = dma("gpsimd", self.dram["out"].rearrange(
                "p (kc t) -> p kc t", kc=KD)[:, :, ch * TC:(ch + 1) * TC],
                self.px2.rearrange("p (kc t) -> p kc t", kc=KD),
                waits=[fin_ev], sem="sp_out")
            free["px2_cons"] = e
            free["px1"] = g_ev
            free["xch"] = fin_ev

    def _replay(self):
        nc, em = self.nc, self.em

        class CountProxy:
            def __init__(self):
                self.n = 0

            def wait_ge(self, *a, **k):
                return None

            def nop(self, *a, **k):
                return _FakeInst()

            def __getattr__(self, name):
                def f(*a, **k):
                    self.n += 1
                    return _FakeInst()
                return f

        class _FakeInst:
            def then_inc(self, *a, **k):
                return self

        group_end = {}
        for eng in ("vector", "scalar"):
            cp = CountProxy()
            g = 0
            for fn, waits, inc, inc_n in em.streams[eng]:
                fn(cp)
                if inc is not None:
                    g += inc_n
                    group_end[(eng, g)] = cp.n

        ENG_OF_SEM = {"v": "vector", "s": "scalar"}

        class PipeProxy:

            def __init__(self, proxy, sem):
                self._p = proxy
                self._sem = sem
                self._n = 0
                self._ach = 0
                self._last_w = {}
                self._last_r = {}

            def wait_ge(self, *a, **k):
                return self._p.wait_ge(*a, **k)

            def nop(self, *a, **k):
                return self._p.nop(*a, **k)

            def _key(self, ap):
                try:
                    t = ap.tensor
                    rng = getattr(t, "manual_sbuf_range", None)
                    return (t.name, rng)
                except Exception:
                    return None

            def __getattr__(self, name):
                f = getattr(self._p, name)
                if not callable(f):
                    return f

                def overlap(r1, r2):
                    if r1 is None or r2 is None:
                        return True
                    return r1[0] < r2[1] and r2[0] < r1[1]

                def lookup(table, nm, rng):
                    need = 0
                    hit = table.get(nm)
                    if hit is not None:
                        need = hit[1]
                    if rng is not None:
                        for onm, (orng, oidx) in table.items():
                            if onm != nm and overlap(rng, orng):
                                need = max(need, oidx)
                    return need

                def wrapped(*a, **k):
                    aps = [x for x in a if isinstance(x, bass.AP)]
                    aps += [x for x in k.values() if isinstance(x, bass.AP)]
                    need = 0
                    outs = []
                    ins = []
                    for i, x in enumerate(aps):
                        key = self._key(x)
                        if key is None:
                            continue
                        nm, rng = key
                        if i == 0:
                            outs.append(key)
                            need = max(need, lookup(self._last_w, nm, rng),
                                       lookup(self._last_r, nm, rng))
                        else:
                            ins.append(key)
                            need = max(need, lookup(self._last_w, nm, rng))
                    if need > self._ach:
                        self._p.wait_ge(self._sem, need)
                        self._ach = need
                    inst = f(*a, **k)
                    if inst is not None:
                        inst.then_inc(self._sem, 1)
                        self._n += 1
                        for nm, rng in outs:
                            self._last_w[nm] = (rng, self._n)
                        for nm, rng in ins:
                            self._last_r[nm] = (rng, self._n)
                    return inst
                return wrapped

        with contextlib.ExitStack() as stack:
            sems = {name: stack.enter_context(nc.semaphore(f"sem_{name}"))
                    for name in em.count if name not in ("v", "s")}
            pipes = {e: stack.enter_context(nc.semaphore(f"pipe_{e}"))
                     for e in ("vector", "scalar")}

            def xlate(s, tgt):
                eng = ENG_OF_SEM.get(s)
                if eng is None:
                    return (sems[s], tgt)
                return (pipes[eng], group_end[(eng, tgt)] if tgt > 0 else 0)

            with nc.Block() as block:
                def mk(eng_name):
                    def runner(proxy):
                        if eng_name in pipes:
                            proxy = PipeProxy(proxy, pipes[eng_name])
                        ach = {}
                        for fn, waits, inc, inc_n in em.streams[eng_name]:
                            for s, tgt in waits:
                                h, t2 = xlate(s, tgt)
                                if t2 > ach.get(h, -1):
                                    proxy.wait_ge(h, t2)
                                    ach[h] = t2
                            last = fn(proxy)
                            if inc is not None and eng_name not in pipes:
                                assert last is not None, \
                                    f"{eng_name} inc without instruction"
                                last.then_inc(sems[inc], inc_n)
                    return runner
                block.sync(mk("sync"))
                block.tensor(mk("tensor"))
                block.vector(mk("vector"))
                block.scalar(mk("scalar"))
                block.gpsimd(mk("gpsimd"))


def kernel(**inputs):
    maps = host_prepare(inputs)
    if "nc" not in _NC_CACHE:
        _NC_CACHE["nc"] = build_nc()
    nc = _NC_CACHE["nc"]
    res = run_bass_kernel_spmd(nc, maps, core_ids=list(range(8)))
    out = np.empty((NB, L, D), np.float32)
    for core in range(8):
        beta, q = core // 4, core % 4
        o = res.results[core]["out"]
        o = o.reshape(128, KD, TOK).transpose(1, 0, 2).reshape(D, TOK)
        out[beta, q * TOK:(q + 1) * TOK] = o.T
    return out
